# version 36
# baseline (speedup 1.0000x reference)
import numpy as np
import ml_dtypes
from contextlib import ExitStack
import concourse.bass as bass
import concourse.mybir as mybir
from concourse.bass_utils import run_bass_kernel_spmd

F32 = mybir.dt.float32
BF16 = mybir.dt.bfloat16
AF = mybir.ActivationFunctionType
ALU = mybir.AluOpType
AX = mybir.AxisListType.X
I32 = mybir.dt.int32

D = 1024
T = 2048
DI = 4096
NSTATE = 16
NC = 16
KC = 8
KI = 32
MAGIC = float(np.float32(12582912.0))
EPS = 1e-5

ENGINES = ("sync", "tensor", "vector", "scalar", "gpsimd")


class Sched:

    def __init__(self, nc, es):
        self.nc = nc
        self.es = es
        self.ops = {e: [] for e in ENGINES}
        self.counts = {}
        self.sems = {}
        for e in ENGINES:
            self.sems[e] = es.enter_context(nc.semaphore(f"s_{e}"))
            self.counts[e] = 0
        self.dma_slots = {}
        self.queue_slots = {e: set() for e in ENGINES}

    def _sem(self, name):
        if name not in self.sems:
            self.sems[name] = self.es.enter_context(self.nc.semaphore(name))
            self.counts[name] = 0
        return self.sems[name]

    def add(self, eng, fn, waits=(), inc=0):
        if fn is not None and eng in ("vector", "scalar", "gpsimd"):
            inc = 1
        self.ops[eng].append([fn, tuple(waits), inc, eng])
        if inc:
            self.counts[eng] += inc
        return (eng, self.counts[eng])

    def dma(self, eng, slot, fn, waits=()):
        name = f"dma_{slot}"
        self._sem(name)
        self.queue_slots[eng].add(name)
        self.ops[eng].append([fn, tuple(waits), 16, name])
        self.counts[name] += 16
        return (name, self.counts[name])

    def token(self, eng):
        return (eng, self.counts[eng])

    def _strong_inc(self, eng):
        if eng in ("sync", "gpsimd"):
            w = [(s, self.counts[s]) for s in sorted(self.queue_slots[eng])]
            if eng == "gpsimd":
                w.append((eng, self.counts[eng]))
            self.add(eng, None, waits=w)
            return self.add(eng, lambda h: h.nop(), inc=1)
        if eng == "tensor":
            for op in reversed(self.ops[eng]):
                if op[0] is not None:
                    assert op[2] >= 1, "tensor barrier crossing an un-inc'd PE op"
                    break
        return (eng, self.counts[eng])

    def chain(self, eng):
        return (eng, self.counts[eng])

    def pe_fence(self):
        self.add("tensor", None, waits=[self._strong_inc("tensor")])

    def barrier(self):
        toks = {e: self._strong_inc(e) for e in ENGINES}
        for e in ENGINES:
            self.add(e, None, waits=[toks[o] for o in ENGINES if o != e])

    def emit(self, block):
        sems = self.sems

        def emit_engine(handle, eng):
            for fn, waits, inc, inc_sem in self.ops[eng]:
                for (wname, wval) in waits:
                    if wval > 0:
                        handle.wait_ge(sems[wname], wval)
                if fn is not None:
                    instr = fn(handle)
                    if inc:
                        instr.then_inc(sems[inc_sem], inc)

        @block.sync
        def _(h):
            emit_engine(h, "sync")

        @block.tensor
        def _(h):
            emit_engine(h, "tensor")

        @block.vector
        def _(h):
            emit_engine(h, "vector")

        @block.scalar
        def _(h):
            emit_engine(h, "scalar")

        @block.gpsimd
        def _(h):
            emit_engine(h, "gpsimd")


def build_nc(stop_phase=99):
    nc = bass.Bass("TRN2", target_bir_lowering=False)

    d_x = nc.dram_tensor("x", [T, D], F32, kind="ExternalInput").ap()
    d_wd = nc.dram_tensor("wd", [D, D], F32, kind="ExternalInput").ap()
    d_bd = nc.dram_tensor("bd", [128, KC], F32, kind="ExternalInput").ap()
    d_alog = nc.dram_tensor("alog", [128, KC * NSTATE], F32, kind="ExternalInput").ap()
    d_bprm = nc.dram_tensor("bprm", [128, KC * NSTATE], F32, kind="ExternalInput").ap()
    d_wo = nc.dram_tensor("wo", [D, D], F32, kind="ExternalInput").ap()
    d_borep = nc.dram_tensor("borep", [128, D], F32, kind="ExternalInput").ap()
    d_wg = nc.dram_tensor("wg", [DI, D], F32, kind="ExternalInput").ap()
    d_wu = nc.dram_tensor("wu", [DI, D], F32, kind="ExternalInput").ap()
    d_wdn = nc.dram_tensor("wdn", [D, DI], F32, kind="ExternalInput").ap()
    d_ident = nc.dram_tensor("ident", [128, 128], BF16, kind="ExternalInput").ap()
    d_const = nc.dram_tensor("constv", [128, 4], F32, kind="ExternalInput").ap()
    d_onesk = nc.dram_tensor("onesk", [1, 128], F32, kind="ExternalInput").ap()
    d_onesp = nc.dram_tensor("onesp", [128, 1], F32, kind="ExternalInput").ap()
    d_out = nc.dram_tensor("out", [T, D], F32, kind="ExternalOutput").ap()
    d_tgT = nc.dram_tensor("tgT", [D, DI], BF16, kind="Internal").ap()
    d_tuT = nc.dram_tensor("tuT", [D, DI], BF16, kind="Internal").ap()
    d_tdT = nc.dram_tensor("tdT", [DI, D], BF16, kind="Internal").ap()

    x_r = d_x.rearrange("(c p) d -> p c d", p=128)
    out_r = d_out.rearrange("(c p) d -> p c d", p=128)

    with ExitStack() as es:
        S = lambda n, sh, dt: es.enter_context(nc.sbuf_tensor(n, sh, dt))
        sch = Sched(nc, es)
        ST = sch.token

        s_x1 = S("s_x1", [128, NC, D], BF16)
        s_lnf = S("s_lnf", [128, NC, D], BF16)
        s_dyq = S("s_dyq", [128, KC, T], BF16)
        s_bd = S("s_bd", [128, KC], F32)
        s_nega = S("s_nega", [128, KC, NSTATE], F32)
        s_bprm = S("s_bprm", [128, KC, NSTATE], F32)
        s_borep = S("s_borep", [128, D], BF16)
        s_ident = S("s_ident", [128, 128], BF16)
        s_onesk = S("s_onesk", [1, 128], F32)
        s_onesp = S("s_onesp", [128, 1], F32)
        s_const = S("s_const", [128, 4], F32)
        s_wsc = S("s_wsc", [128, 6], F32)
        s_r1g = S("s_r1g", [128, NC], F32)
        s_r1u = S("s_r1u", [128, NC], F32)
        s_r2d = S("s_r2d", [128, NC], F32)
        s_stats = S("s_stats", [128, 8], F32)
        s_qs = S("s_qs", [128, 4], F32)
        QT = T // 4
        s_abar = S("s_abar", [128, 4, QT], F32)
        s_sig = S("s_sig", [128, 2, QT], BF16)
        s_ytmp = S("s_ytmp", [128, QT], BF16)
        s_state = S("s_state", [128, KC * NSTATE], BF16)
        sC_st = S("sC_st", [128, 8], F32)
        sC_nt = S("sC_nt", [128, 16], F32)
        es_w_slab = ExitStack()
        s_w = es_w_slab.enter_context(nc.sbuf_tensor("s_w", [128, KC, D], BF16))

        lnT_view = s_lnf[:].rearrange("p a b -> p (a b)").rearrange("p (k t) -> p k t", k=KC)
        delta_view = s_dyq[:]

        sch.dma("sync", "consts", lambda h: h.dma_start(out=s_bd[:], in_=d_bd))
        sch.dma("sync", "consts", lambda h: h.dma_start(out=s_bprm[:], in_=d_bprm))
        sch.dma("gpsimd", "borep", lambda h: h.dma_start(out=s_borep[:], in_=d_borep))
        sch.dma("sync", "consts", lambda h: h.dma_start(out=s_ident[:], in_=d_ident))
        sch.dma("sync", "consts", lambda h: h.dma_start(out=s_onesk[:], in_=d_onesk))
        sch.dma("sync", "consts", lambda h: h.dma_start(out=s_onesp[:], in_=d_onesp))
        sch.dma("sync", "consts", lambda h: h.dma_start(
            out=s_nega[:].rearrange("p k n -> p (k n)"), in_=d_alog))
        t_x0 = sch.dma("gpsimd", "x0", lambda h: h.dma_start(out=s_x1[:], in_=x_r))
        t_consts = sch.dma("sync", "consts", lambda h: h.dma_start(out=s_const[:], in_=d_const))
        t_nega_ld = t_consts
        t_magic = t_consts
        s_magic_ap = s_const[:, 0:1]
        s_eps_ap = s_const[:, 1:2]
        s_negh_ap = s_const[:, 2:3]
        s_negm_ap = s_const[:, 3:4]

        with nc.sbuf_tensor("aw1", [128, KC * NSTATE], F32) as aw1, \
             nc.sbuf_tensor("aw2", [128, KC * NSTATE], F32) as aw2:
            av = s_nega[:].rearrange("p k n -> p (k n)")
            sch.add("scalar", lambda h: h.activation(out=aw1[:], in_=av, func=AF.Abs),
                    waits=[t_nega_ld])
            sch.add("scalar", lambda h: h.activation(out=aw1[:], in_=aw1[:], func=AF.Exp, scale=-1.0),
                    waits=[sch.chain("scalar")])
            sch.add("scalar", lambda h: h.activation(out=aw1[:], in_=aw1[:], func=AF.Ln, bias=1.0),
                    waits=[sch.chain("scalar")])
            sch.add("scalar", lambda h: h.activation(out=aw2[:], in_=av, func=AF.Relu), inc=1)
            sch.add("vector", lambda h: h.tensor_tensor(aw1[:], aw1[:], aw2[:], ALU.add),
                    waits=[ST("scalar")])
            sch.add("vector", lambda h: h.tensor_scalar(av, aw1[:], 1e-4, -1.0, ALU.add, ALU.mult),
                    waits=[sch.chain("vector")])
            sch.barrier()

        def _finish():
            block = es.enter_context(nc.Block())
            sch.emit(block)
            return nc

        if stop_phase <= 0:
            return _finish()

        def prep_square(d_wsrc, name, barrier=True, dma_waits=(), first_tr_waits=()):
            with nc.sbuf_tensor(f"{name}_nat", [128, KC, D], BF16) as wnat, \
                 nc.psum_tensor(f"{name}_pt", [128, KC, 1024], BF16) as pt:
                t_w = sch.dma("gpsimd", f"{name}_nat",
                              lambda h: h.dma_start(out=wnat[:],
                                                    in_=d_wsrc.rearrange("(ec p) d -> p ec d", p=128)),
                              waits=list(dma_waits))
                for dc in range(KC):
                    for ec in range(KC):
                        w = []
                        if dc == 0 and ec == 0:
                            w.append(t_w)
                            w += list(first_tr_waits)
                        if dc > 0 and ec == 0:
                            w.append(ST("scalar"))
                        sch.add("tensor",
                                lambda h, a=ec, b=dc: h.transpose(
                                    pt[:, a, 0:128], wnat[:, a, b * 128:(b + 1) * 128], s_ident[:]),
                                waits=w, inc=(1 if ec == KC - 1 else 0))
                    sch.add("scalar",
                            lambda h, b=dc: h.activation(
                                out=s_w[:, b, :].rearrange("p (k q) -> p k q", q=128),
                                in_=pt[:, :, 0:128], func=AF.Copy),
                            waits=[ST("tensor")], inc=1)
                if barrier:
                    sch.barrier()

        prep_square(d_wd, "wd")
        if stop_phase <= 1:
            return _finish()

        scan_done = {}
        pend_saves = []

        def flush_save(force=False):
            while pend_saves and (force or len(pend_saves) > 1):
                c_s, p_s, tok = pend_saves.pop(0)
                sch.add("gpsimd",
                        lambda h, c_=c_s, p=p_s: h.tensor_tensor(
                            s_state[:, c_:c_ + 1], s_sig[:, p, QT - 1:QT], s_onesp[:],
                            ALU.mult),
                        waits=[tok, sch.chain("gpsimd")])
            return ST("gpsimd")

        def emit_B_dc(q, dc, inject=None, w_extra=()):
            t0 = q * QT
            for n in range(NSTATE):
                pa = n % 4
                ps = n % 2
                col = dc * NSTATE + n
                if n >= 4:
                    key = (dc, n - 4)
                else:
                    key = ((dc - 1) % KC, NSTATE - 4 + n)
                w_buf = [scan_done[key]] if key in scan_done else []
                t_pool = flush_save()
                sch.add("scalar",
                        lambda h, d=dc, n_=n, p=pa: h.activation(
                            out=s_abar[:, p, :], in_=delta_view[:, d, t0:t0 + QT],
                            func=AF.Exp, scale=s_nega[:, d, n_:n_ + 1]),
                        waits=w_buf + (list(w_extra) if n == 0 else []) + [sch.chain("scalar")])
                t_exp = ST("scalar")
                if q > 0:
                    scan_done[(dc, n)] = sch.add("vector",
                            lambda h, d=dc, c_=col, p=pa, s=ps: h.tensor_tensor_scan(
                                s_sig[:, s, :], s_abar[:, p, :],
                                lnT_view[:, d, t0:t0 + QT], s_state[:, c_:c_ + 1],
                                ALU.mult, ALU.add),
                            waits=[t_exp, t_pool, sch.chain("vector")])
                else:
                    scan_done[(dc, n)] = sch.add("vector",
                            lambda h, d=dc, p=pa, s=ps: h.tensor_tensor_scan(
                                s_sig[:, s, :], s_abar[:, p, :],
                                lnT_view[:, d, t0:t0 + QT], 0.0,
                                ALU.mult, ALU.add),
                            waits=[t_exp, t_pool, sch.chain("vector")])
                if q < 3:
                    pend_saves.append((col, ps, scan_done[(dc, n)]))
                if n == 0:
                    sch.add("vector", lambda h, d=dc, n_=n, s=ps: h.tensor_scalar(
                        s_ytmp[:], s_sig[:, s, :], s_bprm[:, d, n_:n_ + 1], None, ALU.mult),
                        waits=[sch.chain("vector")])
                elif n == NSTATE - 1:
                    sch.add("vector", lambda h, d=dc, n_=n, s=ps: h.scalar_tensor_tensor(
                        delta_view[:, d, t0:t0 + QT], s_sig[:, s, :],
                        s_bprm[:, d, n_:n_ + 1], s_ytmp[:], ALU.mult, ALU.add),
                        waits=[sch.chain("vector")])
                else:
                    sch.add("vector", lambda h, d=dc, n_=n, s=ps: h.scalar_tensor_tensor(
                        s_ytmp[:], s_sig[:, s, :], s_bprm[:, d, n_:n_ + 1], s_ytmp[:],
                        ALU.mult, ALU.add),
                        waits=[sch.chain("vector")])
                if inject is not None:
                    inject()


        def _prep_geom(d_wsrc, O, I):
            RC = O // 128
            CC = I // 1024
            IC = I // 128
            _wv = d_wsrc.rearrange("(b p) (c i) -> p b c i", p=128, i=1024)
            if CC == 1:
                wv4_slice = lambda q: _wv[:, 2 * q:2 * q + 2, 0, :]
            else:
                wv4_slice = lambda q: _wv[:, q // 2, 2 * (q % 2):2 * (q % 2) + 2, :]
            return RC, CC, IC, wv4_slice

        def prep_t_pass1(d_wsrc, d_wT, O, I, wsc_col, name, es_w):
            RC, CC, IC, wv4_slice = _prep_geom(d_wsrc, O, I)
            Sx = lambda n, sh, dt: es_w.enter_context(nc.sbuf_tensor(n, sh, dt))
            RING = 4 if CC == 1 else 2
            wbuf = Sx(f"{name}_b1", [128, RING, 2, 1024], F32)
            acc = Sx(f"{name}_acc", [128, 2], F32)
            dum = Sx(f"{name}_dum", [128, 2048], BF16)
            sc2 = Sx(f"{name}_sc2", [1, 2], F32)
            sch.add("gpsimd", lambda h: h.memset(acc[:, 0:1], 0.0),
                    waits=[ST("scalar"), sch.chain("gpsimd")])
            t_abs = {}
            for q in range(RC * CC // 2):
                par = q % RING
                if q >= RING:
                    w_dma = [t_abs[q - RING]]
                else:
                    w_dma = [ST("gpsimd"), ST("scalar")]
                sch.dma("sync", f"wbuf{par}",
                        lambda h, q_=q, p=par: h.dma_start(
                            out=wbuf[:, p, :, :], in_=wv4_slice(q_)),
                        waits=w_dma)
                t_d = (f"dma_wbuf{par}", sch.counts[f"dma_wbuf{par}"])
                sch.add("scalar",
                        lambda h, p=par: h.activation(
                            out=dum[:],
                            in_=wbuf[:, p, :, :].rearrange("p a b -> p (a b)"),
                            func=AF.Abs, accum_out=acc[:, 1:2]),
                        waits=[t_d, ST("gpsimd"), sch.chain("scalar")])
                t_abs[q] = ST("scalar")
                sch.add("gpsimd", lambda h: h.tensor_tensor(acc[:, 0:1], acc[:, 0:1],
                                                            acc[:, 1:2], ALU.add),
                        waits=[ST("scalar"), sch.chain("gpsimd")])
                yield
            with nc.psum_tensor(f"{name}_p1", [1, 1], F32) as p1, \
                 nc.psum_tensor(f"{name}_p2", [128, 2], F32) as p2:
                sch.pe_fence()
                sch.add("tensor", lambda h: h.matmul(p1[:], acc[:, 0:1], s_onesp[:],
                                                     start=True, stop=True),
                        waits=[ST("gpsimd")], inc=1)
                sch.add("scalar", lambda h: h.copy(out=sc2[:, 0:1], in_=p1[:]),
                        waits=[ST("tensor"), sch.chain("scalar")])
                sch.add("vector", lambda h: h.tensor_scalar(sc2[:, 0:1], sc2[:, 0:1],
                                                            1.0 / (O * I), 1e-5,
                                                            ALU.mult, ALU.max),
                        waits=[ST("scalar"), sch.chain("vector")])
                sch.add("vector", lambda h: h.reciprocal(sc2[:, 1:2], sc2[:, 0:1]),
                        waits=[sch.chain("vector")])
                sch.add("tensor", lambda h: h.matmul(p2[:], s_onesk[:], sc2[:],
                                                     start=True, stop=True),
                        waits=[ST("vector")], inc=1)
                sch.add("scalar", lambda h, wc=wsc_col: h.activation(
                    out=s_wsc[:, wc:wc + 2], in_=p2[:], func=AF.Copy),
                    waits=[ST("tensor"), sch.chain("scalar")])
                sch.add("tensor", None, waits=[ST("scalar")])
                sch.pe_fence()
            yield

        def prep_t_pass2(d_wsrc, d_wT, O, I, wsc_col, name, es_w):
            RC, CC, IC, wv4_slice = _prep_geom(d_wsrc, O, I)
            wTv = d_wT.rearrange("(ic p) o -> p ic o", p=128)
            Sx = lambda n, sh, dt: es_w.enter_context(nc.sbuf_tensor(n, sh, dt))
            RING = 4 if CC == 1 else 2
            wbuf = Sx(f"{name}_b2", [128, RING, 2, 1024], F32)
            tern = Sx(f"{name}_tern", [128, 2, I], BF16)
            stage = Sx(f"{name}_stage", [128, IC, 256], BF16)
            swcol = s_wsc[:, wsc_col + 1:wsc_col + 2]
            stage_v = stage[:].rearrange("p ic (two q) -> p ic two q", two=2)
            with nc.psum_tensor(f"{name}_pta", [128, 2, 1024], BF16) as pta, \
                 nc.psum_tensor(f"{name}_ptb", [128, 2, 1024], BF16) as ptb:
                pts4 = (pta, ptb)
                TPD = max(2 // CC, 1)
                bi = 0
                t_pe_hist = {}
                evac_args = {}
                evac_tok = {}

                stage_guard = [None]

                def emit_evac(k):
                    half_, bb_ = evac_args[k]
                    w = [t_pe_hist[k], sch.chain("scalar")]
                    if stage_guard[0] is not None:
                        w.append(stage_guard[0])
                        stage_guard[0] = None
                    sch.add("scalar",
                            (lambda p, b, pp: lambda h: h.activation(
                                out=stage_v[:, b * 2:(b + 1) * 2, p, :],
                                in_=pts4[pp][:, :, 0:128], func=AF.Copy))(half_, bb_, k % 2),
                            waits=w)
                    evac_tok[k] = ST("scalar")

                t_sign = {}
                for rp in range(RC // 2):
                    for half in range(2):
                        r = rp * 2 + half
                        for c in range(CC):
                            gt = r * CC + c
                            if gt % 2 == 0:
                                q = gt // 2
                                par4 = q % RING
                                sch.dma("sync", f"wbuf{par4}",
                                        lambda h, q_=q, p=par4: h.dma_start(
                                            out=wbuf[:, p, :, :], in_=wv4_slice(q_)),
                                        waits=[t_sign[q - RING] if q >= RING
                                               else ST("scalar")])
                            q = gt // 2
                            par4 = q % RING
                            tl = gt % 2
                            t_d = (f"dma_wbuf{par4}", sch.counts[f"dma_wbuf{par4}"])
                            sch.add("scalar",
                                    lambda h, p=par4, t_=tl: h.activation(
                                        out=wbuf[:, p, t_, :], in_=wbuf[:, p, t_, :],
                                        func=AF.Identity, scale=swcol, bias=s_magic_ap),
                                    waits=[t_d, t_magic, ST("tensor"), sch.chain("scalar")])
                            sch.add("scalar",
                                    lambda h, p=half, c_=c, pp=par4, t_=tl: h.activation(
                                        out=tern[:, p, c_ * 1024:(c_ + 1) * 1024],
                                        in_=wbuf[:, pp, t_, :], func=AF.Sign, bias=s_negm_ap),
                                    waits=[ST("tensor"), sch.chain("scalar")])
                            if tl == 1:
                                t_sign[q] = ST("scalar")
                            yield
                        ternv = tern[:].rearrange("p b (ic q) -> p b ic q", q=128)
                        t_q = ST("scalar")
                        for bb in range(IC // 2):
                            ppar = bi % 2
                            for j in range(2):
                                w = []
                                if j == 0:
                                    w.append(t_q)
                                    if bi >= 2:
                                        w.append(evac_tok[bi - 2])
                                sch.add("tensor",
                                        lambda h, p=half, i=bb * 2 + j, j_=j, pp=ppar: h.transpose(
                                            pts4[pp][:, j_, 0:128], ternv[:, p, i, :], s_ident[:]),
                                        waits=w, inc=1)
                            t_pe_hist[bi] = ST("tensor")
                            evac_args[bi] = (half, bb)
                            if bi >= 1 and (bi - 1) not in evac_tok:
                                emit_evac(bi - 1)
                            bi += 1
                            yield
                    if (bi - 1) not in evac_tok:
                        emit_evac(bi - 1)
                    t_ev = ST("scalar")
                    t_st2 = sch.dma("gpsimd", "stage_st",
                                    lambda h, rp_=rp: h.dma_start(
                                        out=wTv[:, :, rp_ * 256:(rp_ + 1) * 256], in_=stage[:]),
                                    waits=[t_ev])
                    stage_guard[0] = t_st2
                    yield
            sch.add("scalar", None, waits=[stage_guard[0]])
            stage_guard[0] = None
            sch.pe_fence()

        def make_prep_driver(genfn, queue):
            import contextlib
            state = {"stack": None, "gen": None, "queue": list(queue)}

            def step():
                while True:
                    if state["gen"] is None:
                        if not state["queue"]:
                            return False
                        args = state["queue"].pop(0)
                        state["stack"] = contextlib.ExitStack()
                        state["gen"] = genfn(*args, state["stack"])
                    try:
                        next(state["gen"])
                        return True
                    except StopIteration:
                        state["stack"].close()
                        state["gen"] = None
                        state["stack"] = None

            return step

        prep_p1 = make_prep_driver(prep_t_pass1,
                                   [(d_wg, d_tgT, DI, D, 0, "wg"),
                                    (d_wu, d_tuT, DI, D, 2, "wu"),
                                    (d_wdn, d_tdT, D, DI, 4, "wdn")])
        prep_step = make_prep_driver(prep_t_pass2,
                                     [(d_wg, d_tgT, DI, D, 0, "wg"),
                                      (d_wu, d_tuT, DI, D, 2, "wu")])
        prep_step2 = make_prep_driver(prep_t_pass2,
                                      [(d_wdn, d_tdT, D, DI, 4, "wdn")])


        def emit_newton4(st4, nt, t_sums):
            sch.add("vector", lambda h: h.tensor_scalar(
                nt[:, 0:4], st4[:, 0:4], 1.0 / D, None, ALU.mult),
                waits=[t_sums, sch.chain("vector")])
            sch.add("vector", lambda h: h.tensor_tensor(
                nt[:, 4:8], st4[:, 0:4], nt[:, 0:4], ALU.mult),
                waits=[sch.chain("vector")])
            sch.add("vector", lambda h: h.tensor_tensor(
                nt[:, 4:8], st4[:, 4:8], nt[:, 4:8], ALU.subtract),
                waits=[sch.chain("vector")])
            sch.add("vector", lambda h: h.tensor_scalar(
                nt[:, 4:8], nt[:, 4:8], 1.0 / D, EPS, ALU.mult, ALU.add),
                waits=[sch.chain("vector")])
            iv = nt[:, 4:8].bitcast(I32)
            iy = nt[:, 8:12].bitcast(I32)
            sch.add("vector", lambda h: h.tensor_scalar(
                iy, iv, 1, None, ALU.logical_shift_right),
                waits=[sch.chain("vector")])
            sch.add("vector", lambda h: h.tensor_scalar(
                iy, iy, -1, 0x5F3759DF, ALU.mult, ALU.add),
                waits=[sch.chain("vector")])
            for _ in range(3):
                sch.add("vector", lambda h: h.tensor_tensor(
                    nt[:, 12:16], nt[:, 8:12], nt[:, 8:12], ALU.mult),
                    waits=[sch.chain("vector")])
                sch.add("vector", lambda h: h.tensor_tensor(
                    nt[:, 12:16], nt[:, 12:16], nt[:, 4:8], ALU.mult),
                    waits=[sch.chain("vector")])
                sch.add("vector", lambda h: h.tensor_scalar(
                    nt[:, 12:16], nt[:, 12:16], -0.5, 1.5, ALU.mult, ALU.add),
                    waits=[sch.chain("vector")])
                sch.add("vector", lambda h: h.tensor_tensor(
                    nt[:, 8:12], nt[:, 8:12], nt[:, 12:16], ALU.mult),
                    waits=[sch.chain("vector")])
            sch.add("vector", lambda h: h.tensor_tensor(
                nt[:, 0:4], nt[:, 0:4], nt[:, 8:12], ALU.mult),
                waits=[sch.chain("vector")])
            sch.add("vector", lambda h: h.tensor_scalar(
                nt[:, 0:4], nt[:, 0:4], -1.0, None, ALU.mult),
                waits=[sch.chain("vector")])
            return ST("vector")

        def emit_lnstats(src_ap, dummy, waits=()):
            sch.add("scalar", lambda h: h.activation(out=dummy[:], in_=src_ap, func=AF.Identity,
                                                     accum_out=s_stats[:, 0:1]), waits=list(waits))
            sch.add("scalar", lambda h: h.activation(out=dummy[:], in_=src_ap, func=AF.Square,
                                                     accum_out=s_stats[:, 1:2]),
                    waits=[sch.chain("scalar")])
            t_s = ST("scalar")
            sch.add("vector", lambda h: h.tensor_scalar(s_stats[:, 2:3], s_stats[:, 0:1],
                                                        1.0 / D, None, ALU.mult),
                    waits=[t_s, sch.chain("vector")])
            sch.add("vector", lambda h: h.scalar_tensor_tensor(s_stats[:, 3:4], s_stats[:, 0:1],
                                                               s_stats[:, 2:3], s_stats[:, 1:2],
                                                               ALU.mult, ALU.subtract),
                    waits=[sch.chain("vector")])
            sch.add("vector", lambda h: h.tensor_scalar(s_stats[:, 3:4], s_stats[:, 3:4],
                                                        -1.0 / D, None, ALU.mult),
                    waits=[sch.chain("vector")])
            t_v = ST("vector")
            sch.add("scalar", lambda h: h.activation(out=s_stats[:, 6:7], in_=s_stats[:, 3:4],
                                                     func=AF.Ln, bias=s_eps_ap), waits=[t_v])
            sch.add("scalar", lambda h: h.activation(out=s_stats[:, 4:5], in_=s_stats[:, 6:7],
                                                     func=AF.Exp, scale=s_negh_ap),
                    waits=[sch.chain("scalar")], inc=1)
            t_r = ST("scalar")
            sch.add("vector", lambda h: h.scalar_tensor_tensor(s_stats[:, 5:6], s_stats[:, 2:3],
                                                               -1.0, s_stats[:, 4:5],
                                                               ALU.mult, ALU.mult),
                    waits=[t_r, sch.chain("vector")])
            return ST("vector")

        with nc.sbuf_tensor("lnscA", [128, D], F32) as lnscA, \
             nc.sbuf_tensor("lnbfA", [128, D], BF16) as lnbfA, \
             nc.sbuf_tensor("spA0", [128, 512], F32) as spA0, \
             nc.sbuf_tensor("spA1", [128, 512], F32) as spA1:
            with nc.psum_tensor("ptA", [128, KC, 1024], BF16) as ptA:
                for tc in range(NC):
                    x0c = s_x1[:, tc, :]
                    t_st = emit_lnstats(x0c, lnscA, waits=[t_x0] if tc == 0 else ())
                    sch.add("scalar",
                            lambda h, src=x0c: h.activation(out=lnbfA[:], in_=src, func=AF.Identity,
                                                            scale=s_stats[:, 4:5], bias=s_stats[:, 5:6]),
                            waits=[t_st], inc=1)
                    t_ln = ST("scalar")
                    for ic in range(KC):
                        w = []
                        if ic == 0:
                            w.append(t_ln)
                            if tc >= 1:
                                w.append(ST("scalar"))
                        sch.add("tensor",
                                lambda h, i=ic: h.transpose(
                                    ptA[:, i, 0:128], lnbfA[:, i * 128:(i + 1) * 128], s_ident[:]),
                                waits=w, inc=(1 if ic == KC - 1 else 0))
                    t_pe = ST("tensor")
                    sch.add("scalar",
                            lambda h, t_=tc: h.activation(
                                out=lnT_view[:, :, t_ * 128:(t_ + 1) * 128],
                                in_=ptA[:, :, 0:128], func=AF.Copy),
                            waits=[t_pe], inc=1)
            if stop_phase <= 1.5:
                return _finish()
            sch.pe_fence()

            with nc.psum_tensor("pdA0", [128, 512], F32) as pdA0, \
                 nc.psum_tensor("pdA1", [128, 512], F32) as pdA1:
                psd = (pdA0, pdA1)
                bq0 = iter(range(KC))
                t_ux0 = None
                t_soft = {}
                for g in range(4):
                    for ec in range(KC):
                        if g >= 1 and ec % 3 == 0:
                            dcb = next(bq0, None)
                            if dcb is not None:
                                emit_B_dc(0, dcb, w_extra=[t_ux0])
                        prep_p1()
                        prep_p1()
                        it = g * KC + ec
                        par = it % 2
                        for kc in range(KC):
                            w = []
                            if kc == 0:
                                w.append(t_soft.get(it - 2, ST("scalar")))
                            sch.add("tensor",
                                    lambda h, p=par, e=ec, k=kc, g_=g: h.matmul(
                                        psd[p][:], s_w[:, k, e * 128:(e + 1) * 128],
                                        lnT_view[:, k, g_ * 512:(g_ + 1) * 512],
                                        start=(k == 0), stop=(k == KC - 1)),
                                    waits=w, inc=(1 if kc == KC - 1 else 0))
                        t_mm = ST("tensor")
                        sch.add("scalar",
                                lambda h, p=par, e=ec: h.activation(out=spA0[:], in_=psd[p][:],
                                                                    func=AF.Abs, bias=s_bd[:, e:e + 1]),
                                waits=[t_mm])
                        sch.add("scalar", lambda h: h.activation(out=spA0[:], in_=spA0[:],
                                                                 func=AF.Exp, scale=-1.0),
                                waits=[sch.chain("scalar")])
                        sch.add("scalar", lambda h: h.activation(out=spA0[:], in_=spA0[:],
                                                                 func=AF.Ln, bias=1.0),
                                waits=[sch.chain("scalar")])
                        sch.add("scalar",
                                lambda h, p=par, e=ec: h.activation(out=spA1[:], in_=psd[p][:],
                                                                    func=AF.Relu, bias=s_bd[:, e:e + 1]),
                                inc=1)
                        t_sp = ST("scalar")
                        t_soft[it] = t_sp
                        sch.add("vector",
                                lambda h, e=ec, g_=g: h.tensor_tensor(
                                    delta_view[:, e, g_ * 512:(g_ + 1) * 512], spA0[:], spA1[:], ALU.add),
                                waits=[t_sp, sch.chain("vector")])
                        sch.add("scalar", None, waits=[ST("vector")])
                    t_mmw = ST("tensor")
                    for d in range(KC):
                        sch.add("vector", lambda h, d_=d, g_=g: h.tensor_tensor(
                            lnT_view[:, d_, g_ * QT:(g_ + 1) * QT],
                            lnT_view[:, d_, g_ * QT:(g_ + 1) * QT],
                            delta_view[:, d_, g_ * QT:(g_ + 1) * QT], ALU.mult),
                            waits=([t_mmw] if d == 0 else []) + [sch.chain("vector")])
                    if g == 0:
                        t_ux0 = ST("vector")
                for dcb in bq0:
                    emit_B_dc(0, dcb, w_extra=[t_ux0])
                while prep_p1():
                    pass

        if stop_phase <= 2:
            return _finish()
        sch.pe_fence()
        prep_square(d_wo, "wo", barrier=False,
                    dma_waits=[ST("scalar"), ST("vector")],
                    first_tr_waits=[ST("scalar")])
        t_wo = ST("scalar")

        flush_save(force=True)
        t_y0 = ST("vector")
        bq1 = iter(range(KC))
        while prep_step():
            dc = next(bq1, None)
            if dc is not None:
                emit_B_dc(1, dc)
        t_prep_done = ST("scalar")

        if stop_phase <= 5:
            return _finish()
        def emit_C_q(q, w_first=(), inject=None):
            tcs = range(4 * q, 4 * q + 4)
            with nc.sbuf_tensor(f"attnC{q}", [128, 4, D], BF16) as attnC4, \
                 nc.sbuf_tensor(f"lnscC{q}", [128, D], F32) as lnscC, \
                 nc.sbuf_tensor(f"q1C{q}", [128, D], BF16) as q1C:
                ln3C = attnC4[:, 0:2, :].bitcast(F32).rearrange("p a b -> p (a b)")
                with nc.psum_tensor(f"pW0{q}", [128, 512], F32) as pW0, \
                     nc.psum_tensor(f"pW1{q}", [128, 512], F32) as pW1:
                    for ci, tc in enumerate(tcs):
                        for half, pw in ((0, pW0), (1, pW1)):
                            for kc in range(KC):
                                w = []
                                if kc == 0 and half == 0:
                                    if tc == 4 * q:
                                        w += list(w_first)
                                    else:
                                        w.append(ST("vector"))
                                sch.add("tensor",
                                        lambda h, k=kc, hf=half, t_=tc, p=pw: h.matmul(
                                            p[:], delta_view[:, k, t_ * 128:(t_ + 1) * 128],
                                            s_w[:, k, hf * 512:(hf + 1) * 512],
                                            start=(k == 0), stop=(k == KC - 1)),
                                        waits=w, inc=(1 if (half, kc) == (1, KC - 1) else 0))
                        t_mm = ST("tensor")
                        sch.add("vector", lambda h, c_=ci: h.tensor_tensor(
                            attnC4[:, c_, 0:512], pW0[:], s_borep[:, 0:512], ALU.add),
                            waits=[t_mm, ("dma_borep", 16), sch.chain("vector")])
                        sch.add("vector", lambda h, c_=ci: h.tensor_tensor(
                            attnC4[:, c_, 512:1024], pW1[:], s_borep[:, 512:1024], ALU.add),
                            waits=[sch.chain("vector")])
                        t_attn = ST("vector")
                        sch.add("scalar", lambda h, c_=ci: h.activation(
                            out=lnscC[:], in_=attnC4[:, c_, :], func=AF.Identity,
                            accum_out=sC_st[:, c_:c_ + 1]),
                            waits=[t_attn, sch.chain("scalar")])
                        sch.add("scalar", lambda h, c_=ci: h.activation(
                            out=lnscC[:], in_=attnC4[:, c_, :], func=AF.Square,
                            accum_out=sC_st[:, 4 + c_:5 + c_]),
                            waits=[sch.chain("scalar")])
                    t_rs2 = emit_newton4(sC_st, sC_nt, ST("scalar"))
                    for ci, tc in enumerate(tcs):
                        sch.add("scalar", lambda h, c_=ci: h.activation(
                            out=lnscC[:], in_=attnC4[:, c_, :], func=AF.Identity,
                            scale=sC_nt[:, 8 + c_:9 + c_], bias=sC_nt[:, c_:c_ + 1]),
                            waits=([t_rs2] if ci == 0 else []) + [sch.chain("scalar")])
                        t_ln2 = ST("scalar")
                        sch.add("vector", lambda h, t_=tc: h.tensor_tensor(
                            s_x1[:, t_, :], s_x1[:, t_, :], lnscC[:], ALU.add),
                            waits=[t_ln2, sch.chain("vector")])
                        sch.add("scalar", None, waits=[ST("vector")])
                sch.pe_fence()
                with nc.psum_tensor(f"ptC{q}", [128, 4, 1024], BF16) as ptC:
                    for ci, tc in enumerate(tcs):
                        sch.add("scalar", lambda h, t_=tc, c_=ci: h.activation(
                            out=lnscC[:], in_=s_x1[:, t_, :], func=AF.Identity,
                            accum_out=sC_st[:, c_:c_ + 1]),
                            waits=[sch.chain("scalar")])
                        sch.add("scalar", lambda h, t_=tc, c_=ci: h.activation(
                            out=lnscC[:], in_=s_x1[:, t_, :], func=AF.Square,
                            accum_out=sC_st[:, 4 + c_:5 + c_]),
                            waits=[sch.chain("scalar")])
                    t_rs3 = emit_newton4(sC_st, sC_nt, ST("scalar"))
                    for ci, tc in enumerate(tcs):
                        sch.add("scalar",
                                lambda h, t_=tc, c_=ci: h.activation(
                                    out=ln3C, in_=s_x1[:, t_, :], func=AF.Identity,
                                    scale=sC_nt[:, 8 + c_:9 + c_], bias=sC_nt[:, c_:c_ + 1]),
                                waits=([t_rs3] if ci == 0 else []) + [sch.chain("scalar")])
                        t_ln3 = ST("scalar")
                        sch.add("vector", lambda h: h.tensor_reduce(
                            s_qs[:, 0:1], ln3C[:], AX, ALU.max, apply_absolute_value=True),
                            waits=[t_ln3, sch.chain("vector")])
                        sch.add("vector", lambda h: h.tensor_scalar(
                            s_qs[:, 0:1], s_qs[:, 0:1], 1e-5, None, ALU.max),
                            waits=[sch.chain("vector")])
                        sch.add("vector", lambda h: h.reciprocal(s_qs[:, 1:2], s_qs[:, 0:1]),
                                waits=[sch.chain("vector")])
                        sch.add("vector", lambda h: h.tensor_scalar(
                            s_qs[:, 1:2], s_qs[:, 1:2], 127.0, None, ALU.mult),
                            waits=[sch.chain("vector")])
                        sch.add("vector", lambda h, t_=tc: h.tensor_scalar(
                            s_r1g[:, t_:t_ + 1], s_qs[:, 0:1], 1.0 / 127.0, None, ALU.mult),
                            waits=[sch.chain("vector")])
                        t_q = ST("vector")
                        sch.add("scalar", lambda h: h.activation(out=lnscC[:], in_=ln3C[:],
                                                                 func=AF.Identity,
                                                                 scale=s_qs[:, 1:2],
                                                                 bias=s_magic_ap),
                                waits=[t_q], inc=1)
                        sch.add("vector", lambda h: h.tensor_scalar(
                            q1C[:], lnscC[:], MAGIC, None, ALU.subtract),
                            waits=[ST("scalar"), sch.chain("vector")])
                        t_q1 = ST("vector")
                        for bb in range(2):
                            for j in range(4):
                                ic = bb * 4 + j
                                w = []
                                if j == 0:
                                    w.append(t_q1)
                                    w.append(ST("scalar"))
                                sch.add("tensor",
                                        lambda h, i=ic, j_=j: h.transpose(
                                            ptC[:, j_, 0:128], q1C[:, i * 128:(i + 1) * 128],
                                            s_ident[:]),
                                        waits=w, inc=(1 if j == 3 else 0))
                            t_tr = ST("tensor")
                            sch.add("scalar",
                                    lambda h, t_=tc, b=bb: h.activation(
                                        out=delta_view[:, b * 4:(b + 1) * 4,
                                                       t_ * 128:(t_ + 1) * 128],
                                        in_=ptC[:, :, 0:128], func=AF.Copy),
                                    waits=[t_tr, sch.chain("scalar")])
                        if inject is not None:
                            for _ in range(6):
                                inject()
                    if inject is not None:
                        while inject():
                            pass
            sch.pe_fence()
            c0 = 4 * q
            sch.add("vector", lambda h: h.tensor_scalar(
                s_r1u[:, c0:c0 + 4], s_r1g[:, c0:c0 + 4], s_wsc[:, 2:3], None, ALU.mult),
                waits=[sch.chain("vector")])
            sch.add("vector", lambda h: h.tensor_scalar(
                s_r1g[:, c0:c0 + 4], s_r1g[:, c0:c0 + 4], s_wsc[:, 0:1], None, ALU.mult),
                waits=[sch.chain("vector")])
            return ST("scalar")

        def c0_inject():
            r = prep_step2()
            dc = next(bq1, None)
            if dc is not None:
                emit_B_dc(1, dc)
            return r

        t_c_done = emit_C_q(0, w_first=[t_y0, t_prep_done, t_wo], inject=c0_inject)
        for dc in bq1:
            emit_B_dc(1, dc)
        flush_save(force=True)
        t_y_q = {1: ST("vector")}
        bpend = [(2, dc) for dc in range(KC)] + [(3, dc) for dc in range(KC)]

        def pop_B(k):
            while k > 0 and bpend:
                qq, dc = bpend.pop(0)
                emit_B_dc(qq, dc)
                if dc == KC - 1:
                    flush_save(force=True)
                    t_y_q[qq] = ST("vector")
                k -= 1
        sch.barrier()

        if stop_phase <= 6:
            return _finish()
        q1T_view = s_dyq[:]
        f_ap = lambda tc: s_dyq[:, 2 * (tc % 4):2 * (tc % 4) + 2,
                                (tc // 4) * 512:(tc // 4 + 1) * 512]
        tgT_v = d_tgT.rearrange("(kc p) o -> p kc o", p=128)
        tuT_v = d_tuT.rearrange("(kc p) o -> p kc o", p=128)
        tdT_v = d_tdT.rearrange("(kc p) o -> p kc o", p=128).rearrange(
            "p (q f) o -> p q f o", f=2)
        with nc.sbuf_tensor("s_h", [128, 4, DI], BF16) as s_h, \
             nc.sbuf_tensor("s_tg", [128, 2, 4, 512], BF16) as s_tg, \
             nc.sbuf_tensor("s_tu", [128, 2, 4, 512], BF16) as s_tu, \
             nc.sbuf_tensor("s_gtmp", [128, 2, 512], BF16) as s_gtmp, \
             nc.sbuf_tensor("s_q2sc", [128, 4], F32) as s_q2sc, \
             nc.sbuf_tensor("s_q2s", [128, 512], F32) as s_q2s, \
             nc.sbuf_tensor("s_q2b", [128, 512], BF16) as s_q2b, \
             nc.sbuf_tensor("s_td", [128, 2, 2, 1024], BF16) as s_td, \
             nc.sbuf_tensor("s_st4", [128, 8], F32) as s_st4, \
             nc.sbuf_tensor("s_nt", [128, 16], F32) as s_nt, \
             nc.sbuf_tensor("s_out", [128, D], F32) as s_out:
            q2T_view = s_h[:].rearrange("p c (k q) -> p c k q", q=128)
            t_hmult = {}
            prev_t_fg = None
            prev_t_dn = None
            for g in range(4):
                with ExitStack() as es_gu:
                    pGs = tuple(es_gu.enter_context(
                        nc.psum_tensor(f"pG{i}{g}", [128, 512], F32)) for i in range(4))
                    pUs = tuple(es_gu.enter_context(
                        nc.psum_tensor(f"pU{i}{g}", [128, 512], F32)) for i in range(4))
                    t_unit_mms = []
                    t_mm_c = {}
                    for oc in range(8):
                        for half in range(2):
                            un = oc * 2 + half
                            wpar = un % 2
                            w_g = []
                            if len(t_unit_mms) >= 2:
                                w_g.append(t_unit_mms[-2])
                            t_g = sch.dma("sync", f"tg{wpar}",
                                          lambda h, o=oc, hf=half, p=wpar: h.dma_start(
                                              out=s_tg[:, p, :, :],
                                              in_=tgT_v[:, hf * 4:(hf + 1) * 4,
                                                        o * 512:(o + 1) * 512]),
                                          waits=w_g)
                            t_u = sch.dma("sync", f"tu{wpar}",
                                          lambda h, o=oc, hf=half, p=wpar: h.dma_start(
                                              out=s_tu[:, p, :, :],
                                              in_=tuT_v[:, hf * 4:(hf + 1) * 4,
                                                        o * 512:(o + 1) * 512]),
                                          waits=w_g)
                            for c in range(4):
                                tc = g * 4 + c
                                for k4 in range(4):
                                    kc = half * 4 + k4
                                    w = []
                                    if k4 == 0 and c == 0:
                                        w += [t_g, t_u]
                                    if kc == 0:
                                        key = (oc - 1, c)
                                        if key in t_hmult:
                                            w.append(t_hmult[key])
                                        if oc == 0 and c == 0:
                                            w.append(t_c_done)
                                            if g > 0:
                                                w.append(prev_t_fg)
                                    sch.add("tensor",
                                            lambda h, k=kc, t_=tc, c_=c, wp=wpar, k4_=k4:
                                            h.matmul(
                                                pGs[c_][:],
                                                q1T_view[:, k, t_ * 128:(t_ + 1) * 128],
                                                s_tg[:, wp, k4_, :],
                                                start=(k == 0), stop=(k == KC - 1)),
                                            waits=w)
                                    sch.add("tensor",
                                            lambda h, k=kc, t_=tc, c_=c, wp=wpar, k4_=k4:
                                            h.matmul(
                                                pUs[c_][:],
                                                q1T_view[:, k, t_ * 128:(t_ + 1) * 128],
                                                s_tu[:, wp, k4_, :],
                                                start=(k == 0), stop=(k == KC - 1)),
                                            inc=(1 if (k4 == 3 and
                                                       (half == 1 or c == 3)) else 0))
                                if half == 1:
                                    t_mm_c[c] = ST("tensor")
                            t_unit_mms.append(ST("tensor"))
                        for c in range(4):
                            tc = g * 4 + c
                            ppar = c % 2
                            sch.add("scalar",
                                    lambda h, t_=tc, c_=c, p=ppar: h.activation(
                                        out=s_gtmp[:, p, :], in_=pGs[c_][:], func=AF.Silu,
                                        scale=s_r1g[:, t_:t_ + 1]),
                                    waits=([t_mm_c[c], ST("vector")] if c == 0
                                           else [t_mm_c[c], ST("vector"),
                                                 sch.chain("scalar")]))
                            t_ev = ST("scalar")
                            t_hmult[(oc, c)] = sch.add("vector",
                                    lambda h, c_=c, o=oc, p=ppar: h.tensor_tensor(
                                        s_h[:, c_, o * 512:(o + 1) * 512], s_gtmp[:, p, :],
                                        pUs[c_][:], ALU.mult),
                                    waits=[t_ev, sch.chain("vector")])
                            if oc == 7:
                                sch.add("vector", lambda h, c_=c: h.tensor_reduce(
                                    s_qs[:, 0:1], s_h[:, c_, :], AX, ALU.max,
                                    apply_absolute_value=True),
                                    waits=[sch.chain("vector")])
                                sch.add("vector", lambda h: h.tensor_scalar(
                                    s_qs[:, 0:1], s_qs[:, 0:1], 1e-5, None, ALU.max),
                                    waits=[sch.chain("vector")])
                                sch.add("vector", lambda h, c_=c: h.reciprocal(
                                    s_q2sc[:, c_:c_ + 1], s_qs[:, 0:1]),
                                    waits=[sch.chain("vector")])
                                sch.add("vector", lambda h, c_=c: h.tensor_scalar(
                                    s_q2sc[:, c_:c_ + 1], s_q2sc[:, c_:c_ + 1], 127.0, None,
                                    ALU.mult),
                                    waits=[sch.chain("vector")])
                                sch.add("vector", lambda h: h.tensor_scalar(
                                    s_qs[:, 2:3], s_qs[:, 0:1], 1.0 / 127.0, s_wsc[:, 4:5],
                                    ALU.mult, ALU.mult),
                                    waits=[sch.chain("vector")])
                                sch.add("vector", lambda h, t_=tc: h.tensor_tensor(
                                    s_r2d[:, t_:t_ + 1], s_qs[:, 2:3], s_r1u[:, t_:t_ + 1],
                                    ALU.mult),
                                    waits=[sch.chain("vector")])
                        if oc in (3, 6):
                            pop_B(1)
                sch.pe_fence()
                with nc.psum_tensor(f"ptD{g}", [128, 4, 1024], BF16) as ptD:
                    tok_tr = {}
                    for c in range(4):
                        tc = g * 4 + c
                        t_sub = {}
                        for osl in range(8):
                            qpar = osl % 2
                            sch.add("vector",
                                    lambda h, c_=c, o=osl: h.tensor_scalar(
                                        s_q2s[:], s_h[:, c_, o * 512:(o + 1) * 512],
                                        s_q2sc[:, c_:c_ + 1], MAGIC, ALU.mult, ALU.add),
                                    waits=[sch.chain("vector")])
                            w_sub = []
                            if c * 8 + osl >= 1:
                                w_sub.append(tok_tr[c * 8 + osl - 1])
                            w_sub.append(sch.chain("vector"))
                            t_sub[osl] = sch.add("vector",
                                    lambda h: h.tensor_scalar(
                                        s_q2b[:], s_q2s[:], MAGIC, None, ALU.subtract),
                                    waits=w_sub)
                            for j in range(4):
                                w = []
                                if j == 0:
                                    w.append(t_sub[osl])
                                    w.append(ST("scalar"))
                                sch.add("tensor",
                                        lambda h, j_=j: h.transpose(
                                            ptD[:, j_, 0:128], s_q2b[:, j_ * 128:(j_ + 1) * 128],
                                            s_ident[:]),
                                        waits=w, inc=(1 if j == 3 else 0))
                            t_tr = ST("tensor")
                            tok_tr[c * 8 + osl] = t_tr
                            sch.add("scalar",
                                    lambda h, c_=c, o=osl: h.activation(
                                        out=q2T_view[:, c_, o * 4:(o + 1) * 4, :],
                                        in_=ptD[:, :, 0:128], func=AF.Copy),
                                    waits=[t_tr, sch.chain("scalar")])
                t_q2ev = ST("scalar")
                pop_B(6 if g == 0 else 2)
                td_tok = {}
                for q in range(2):
                    w_dma = [prev_t_dn] if g > 0 else []
                    sch.dma("gpsimd", f"td{q}",
                            lambda h, q_=q: h.dma_start(out=s_td[:, q_, :, :],
                                                        in_=tdT_v[:, q_, :, :]),
                            waits=w_dma)
                    td_tok[q] = (f"dma_td{q}", sch.counts[f"dma_td{q}"])
                with nc.psum_tensor(f"pD0{g}", [128, 1024], F32) as pD0, \
                     nc.psum_tensor(f"pD1{g}", [128, 1024], F32) as pD1, \
                     nc.psum_tensor(f"pD2{g}", [128, 1024], F32) as pD2, \
                     nc.psum_tensor(f"pD3{g}", [128, 1024], F32) as pD3:
                    pDs = (pD0, pD1, pD2, pD3)
                    for q in range(KI // 2):
                        par = q % 2
                        if q >= 2:
                            sch.dma("gpsimd", f"td{par}",
                                    lambda h, q_=q, p=par: h.dma_start(
                                        out=s_td[:, p, :, :], in_=tdT_v[:, q_, :, :]),
                                    waits=[ST("tensor")])
                            t_td = (f"dma_td{par}", sch.counts[f"dma_td{par}"])
                        else:
                            t_td = td_tok[q]
                        for sub in range(2):
                            kc = q * 2 + sub
                            for c in range(4):
                                for hf in range(2):
                                    sch.add("tensor",
                                            lambda h, c_=c, k=kc, p=par, s_=sub, f=hf: h.matmul(
                                                pDs[c_][:, f * 512:(f + 1) * 512],
                                                q2T_view[:, c_, k, :],
                                                s_td[:, p, s_, f * 512:(f + 1) * 512],
                                                start=(k == 0), stop=(k == KI - 1)),
                                            waits=(([t_td, t_q2ev] if q == 0 else [t_td])
                                                   if (c == 0 and hf == 0 and sub == 0) else []),
                                            inc=(1 if (c, hf, sub) == (3, 1, 1) else 0))
                    t_dn = ST("tensor")
                    for c in range(4):
                        tc = g * 4 + c
                        sch.add("scalar",
                                lambda h, c_=c, t_=tc: h.activation(
                                    out=f_ap(t_), in_=pDs[c_][:], func=AF.Identity,
                                    scale=s_r2d[:, t_:t_ + 1]),
                                waits=([t_dn] if c == 0 else []) + [sch.chain("scalar")])
                    t_fg = ST("scalar")
                for c in range(4):
                    tc = g * 4 + c
                    sch.add("scalar",
                            lambda h, t_=tc, c_=c: h.activation(
                                out=s_h[:, c_, 0:D], in_=f_ap(t_), func=AF.Identity,
                                accum_out=s_st4[:, c_:c_ + 1]),
                            waits=[t_fg, sch.chain("scalar")])
                    sch.add("scalar",
                            lambda h, t_=tc, c_=c: h.activation(
                                out=s_h[:, c_, 0:D], in_=f_ap(t_), func=AF.Square,
                                accum_out=s_st4[:, 4 + c_:5 + c_]),
                            waits=[sch.chain("scalar")])
                t_sums = ST("scalar")
                sch.add("vector", lambda h: h.tensor_scalar(
                    s_nt[:, 0:4], s_st4[:, 0:4], 1.0 / D, None, ALU.mult),
                    waits=[t_sums, sch.chain("vector")])
                sch.add("vector", lambda h: h.tensor_tensor(
                    s_nt[:, 4:8], s_st4[:, 0:4], s_nt[:, 0:4], ALU.mult),
                    waits=[sch.chain("vector")])
                sch.add("vector", lambda h: h.tensor_tensor(
                    s_nt[:, 4:8], s_st4[:, 4:8], s_nt[:, 4:8], ALU.subtract),
                    waits=[sch.chain("vector")])
                sch.add("vector", lambda h: h.tensor_scalar(
                    s_nt[:, 4:8], s_nt[:, 4:8], 1.0 / D, EPS, ALU.mult, ALU.add),
                    waits=[sch.chain("vector")])
                iv = s_nt[:, 4:8].bitcast(I32)
                iy = s_nt[:, 8:12].bitcast(I32)
                sch.add("vector", lambda h: h.tensor_scalar(
                    iy, iv, 1, None, ALU.logical_shift_right),
                    waits=[sch.chain("vector")])
                sch.add("vector", lambda h: h.tensor_scalar(
                    iy, iy, -1, 0x5F3759DF, ALU.mult, ALU.add),
                    waits=[sch.chain("vector")])
                for _ in range(3):
                    sch.add("vector", lambda h: h.tensor_tensor(
                        s_nt[:, 12:16], s_nt[:, 8:12], s_nt[:, 8:12], ALU.mult),
                        waits=[sch.chain("vector")])
                    sch.add("vector", lambda h: h.tensor_tensor(
                        s_nt[:, 12:16], s_nt[:, 12:16], s_nt[:, 4:8], ALU.mult),
                        waits=[sch.chain("vector")])
                    sch.add("vector", lambda h: h.tensor_scalar(
                        s_nt[:, 12:16], s_nt[:, 12:16], -0.5, 1.5, ALU.mult, ALU.add),
                        waits=[sch.chain("vector")])
                    sch.add("vector", lambda h: h.tensor_tensor(
                        s_nt[:, 8:12], s_nt[:, 8:12], s_nt[:, 12:16], ALU.mult),
                        waits=[sch.chain("vector")])
                sch.add("vector", lambda h: h.tensor_tensor(
                    s_nt[:, 0:4], s_nt[:, 0:4], s_nt[:, 8:12], ALU.mult),
                    waits=[sch.chain("vector")])
                sch.add("vector", lambda h: h.tensor_scalar(
                    s_nt[:, 0:4], s_nt[:, 0:4], -1.0, None, ALU.mult),
                    waits=[sch.chain("vector")])
                t_rs = ST("vector")
                for c in range(4):
                    tc = g * 4 + c
                    w_ap = [t_rs, sch.chain("scalar")]
                    if tc >= 1:
                        w_ap.append(("dma_out0", sch.counts.get("dma_out0", 0)))
                    sch.add("scalar",
                            lambda h, t_=tc, c_=c: h.activation(
                                out=s_out[:], in_=f_ap(t_), func=AF.Identity,
                                scale=s_nt[:, 8 + c_:9 + c_], bias=s_nt[:, c_:c_ + 1]),
                            waits=w_ap)
                    t_ap = ST("scalar")
                    sch.add("vector",
                            lambda h, t_=tc: h.tensor_tensor(
                                s_out[:], s_out[:], s_x1[:, t_, :], ALU.add),
                            waits=[t_ap, sch.chain("vector")])
                    t_o = ST("vector")
                    sch.dma("sync", "out0",
                            lambda h, t_=tc: h.dma_start(out=out_r[:, t_, :],
                                                         in_=s_out[:]),
                            waits=[t_o])
                prev_t_fg = t_fg
                prev_t_dn = t_dn
                if g < 3:
                    t_c_done = emit_C_q(g + 1, w_first=[t_y_q[g + 1], t_fg])
            sch.add("gpsimd", None,
                    waits=[("dma_out0", sch.counts["dma_out0"])])

        es_w_slab.close()
        if True:
            return _finish()
        with nc.sbuf_tensor("lnscT", [128, D], F32) as lnscT, \
             nc.sbuf_tensor("s_out", [128, D], F32) as s_out:
            for tc in range(NC):
                t_st = emit_lnstats(f_view[:, tc, :], lnscT)
                sch.add("scalar",
                        lambda h, t_=tc: h.activation(out=lnscT[:], in_=f_view[:, t_, :],
                                                      func=AF.Identity, scale=s_stats[:, 4:5],
                                                      bias=s_stats[:, 5:6]),
                        waits=[t_st], inc=1)
                t_ln4 = ST("scalar")
                par = tc % 2
                w = [t_ln4]
                if tc >= 2:
                    w.append((f"dma_out{par}", sch.counts.get(f"dma_out{par}", 0)))
                sch.add("vector",
                        lambda h, t_=tc, p=par: h.tensor_tensor(
                            s_out[:, p, :], lnscT[:], s_x1[:, t_, :], ALU.add),
                        waits=w + [sch.chain("vector")])
                t_o = ST("vector")
                sch.dma("sync", f"out{par}",
                        lambda h, t_=tc, p=par: h.dma_start(out=out_r[:, t_, :],
                                                            in_=s_out[:, p, :]),
                        waits=[t_o])
                sch.add("scalar", None, waits=[t_o])
            sch.add("gpsimd", None,
                    waits=[("dma_out0", sch.counts["dma_out0"]),
                           ("dma_out1", sch.counts["dma_out1"])])

        return _finish()


_NC_CACHE = None


def _get_nc():
    global _NC_CACHE
    if _NC_CACHE is None:
        _NC_CACHE = build_nc()
    return _NC_CACHE


def kernel(L0_attn_x, L0_attn_Wd, L0_attn_bd, L0_attn_a_log, L0_attn_b_param,
           L0_attn_Wo, L0_attn_bo, L0_ffn_w_gate, L0_ffn_w_up, L0_ffn_w_down):
    nc = _get_nc()
    x = np.ascontiguousarray(np.asarray(L0_attn_x, dtype=np.float32))
    wd = np.ascontiguousarray(np.asarray(L0_attn_Wd, dtype=np.float32))
    bd = np.ascontiguousarray(np.asarray(L0_attn_bd, dtype=np.float32).reshape(KC, 128).T)
    alog = np.ascontiguousarray(np.asarray(L0_attn_a_log, dtype=np.float32)
                                .reshape(KC, 128, NSTATE).transpose(1, 0, 2).reshape(128, KC * NSTATE))
    bprm = np.ascontiguousarray(np.asarray(L0_attn_b_param, dtype=np.float32)
                                .reshape(KC, 128, NSTATE).transpose(1, 0, 2).reshape(128, KC * NSTATE))
    wo = np.ascontiguousarray(np.asarray(L0_attn_Wo, dtype=np.float32))
    borep = np.ascontiguousarray(np.broadcast_to(np.asarray(L0_attn_bo, dtype=np.float32), (128, D)))
    wg = np.ascontiguousarray(np.asarray(L0_ffn_w_gate, dtype=np.float32))
    wu = np.ascontiguousarray(np.asarray(L0_ffn_w_up, dtype=np.float32))
    wdn = np.ascontiguousarray(np.asarray(L0_ffn_w_down, dtype=np.float32))
    ident = np.eye(128, dtype=ml_dtypes.bfloat16)
    constv = np.ascontiguousarray(np.broadcast_to(
        np.array([MAGIC, EPS, -0.5, -MAGIC], dtype=np.float32), (128, 4)))
    onesk = np.ones((1, 128), dtype=np.float32)
    onesp = np.ones((128, 1), dtype=np.float32)

    in_maps = []
    for b in range(8):
        in_maps.append({
            "x": np.ascontiguousarray(x[b]),
            "wd": wd, "bd": bd, "alog": alog, "bprm": bprm,
            "wo": wo, "borep": borep,
            "wg": wg, "wu": wu, "wdn": wdn,
            "ident": ident, "onesk": onesk, "onesp": onesp, "constv": constv,
        })
    res = run_bass_kernel_spmd(nc, in_maps, core_ids=list(range(8)))
    out = np.stack([res.results[b]["out"] for b in range(8)], axis=0)
    return out.astype(np.float32)


# version 37
# speedup vs baseline: 1.0499x; 1.0499x over previous
import numpy as np
import ml_dtypes
from contextlib import ExitStack
import concourse.bass as bass
import concourse.mybir as mybir
from concourse.bass_utils import run_bass_kernel_spmd

F32 = mybir.dt.float32
BF16 = mybir.dt.bfloat16
AF = mybir.ActivationFunctionType
ALU = mybir.AluOpType
AX = mybir.AxisListType.X
I32 = mybir.dt.int32

D = 1024
T = 2048
DI = 4096
NSTATE = 16
NC = 16
KC = 8
KI = 32
MAGIC = float(np.float32(12582912.0))
EPS = 1e-5

ENGINES = ("sync", "tensor", "vector", "scalar", "gpsimd")


class Sched:

    def __init__(self, nc, es):
        self.nc = nc
        self.es = es
        self.ops = {e: [] for e in ENGINES}
        self.counts = {}
        self.sems = {}
        for e in ENGINES:
            self.sems[e] = es.enter_context(nc.semaphore(f"s_{e}"))
            self.counts[e] = 0
        self.dma_slots = {}
        self.queue_slots = {e: set() for e in ENGINES}

    def _sem(self, name):
        if name not in self.sems:
            self.sems[name] = self.es.enter_context(self.nc.semaphore(name))
            self.counts[name] = 0
        return self.sems[name]

    def add(self, eng, fn, waits=(), inc=0):
        if fn is not None and eng in ("vector", "scalar", "gpsimd"):
            inc = 1
        self.ops[eng].append([fn, tuple(waits), inc, eng])
        if inc:
            self.counts[eng] += inc
        return (eng, self.counts[eng])

    def dma(self, eng, slot, fn, waits=()):
        name = f"dma_{slot}"
        self._sem(name)
        self.queue_slots[eng].add(name)
        self.ops[eng].append([fn, tuple(waits), 16, name])
        self.counts[name] += 16
        return (name, self.counts[name])

    def token(self, eng):
        return (eng, self.counts[eng])

    def _strong_inc(self, eng):
        if eng in ("sync", "gpsimd"):
            w = [(s, self.counts[s]) for s in sorted(self.queue_slots[eng])]
            if eng == "gpsimd":
                w.append((eng, self.counts[eng]))
            self.add(eng, None, waits=w)
            return self.add(eng, lambda h: h.nop(), inc=1)
        if eng == "tensor":
            for op in reversed(self.ops[eng]):
                if op[0] is not None:
                    assert op[2] >= 1, "tensor barrier crossing an un-inc'd PE op"
                    break
        return (eng, self.counts[eng])

    def chain(self, eng):
        return (eng, self.counts[eng])

    def pe_fence(self):
        self.add("tensor", None, waits=[self._strong_inc("tensor")])

    def barrier(self):
        toks = {e: self._strong_inc(e) for e in ENGINES}
        for e in ENGINES:
            self.add(e, None, waits=[toks[o] for o in ENGINES if o != e])

    def emit(self, block):
        sems = self.sems

        def emit_engine(handle, eng):
            for fn, waits, inc, inc_sem in self.ops[eng]:
                for (wname, wval) in waits:
                    if wval > 0:
                        handle.wait_ge(sems[wname], wval)
                if fn is not None:
                    instr = fn(handle)
                    if inc:
                        instr.then_inc(sems[inc_sem], inc)

        @block.sync
        def _(h):
            emit_engine(h, "sync")

        @block.tensor
        def _(h):
            emit_engine(h, "tensor")

        @block.vector
        def _(h):
            emit_engine(h, "vector")

        @block.scalar
        def _(h):
            emit_engine(h, "scalar")

        @block.gpsimd
        def _(h):
            emit_engine(h, "gpsimd")


def build_nc(stop_phase=99):
    nc = bass.Bass("TRN2", target_bir_lowering=False)

    d_x = nc.dram_tensor("x", [T, D], F32, kind="ExternalInput").ap()
    d_wd = nc.dram_tensor("wd", [D, D], F32, kind="ExternalInput").ap()
    d_bd = nc.dram_tensor("bd", [128, KC], F32, kind="ExternalInput").ap()
    d_alog = nc.dram_tensor("alog", [128, KC * NSTATE], F32, kind="ExternalInput").ap()
    d_bprm = nc.dram_tensor("bprm", [128, KC * NSTATE], F32, kind="ExternalInput").ap()
    d_wo = nc.dram_tensor("wo", [D, D], F32, kind="ExternalInput").ap()
    d_borep = nc.dram_tensor("borep", [128, D], F32, kind="ExternalInput").ap()
    d_wg = nc.dram_tensor("wg", [DI, D], F32, kind="ExternalInput").ap()
    d_wu = nc.dram_tensor("wu", [DI, D], F32, kind="ExternalInput").ap()
    d_wdn = nc.dram_tensor("wdn", [D, DI], F32, kind="ExternalInput").ap()
    d_ident = nc.dram_tensor("ident", [128, 128], BF16, kind="ExternalInput").ap()
    d_const = nc.dram_tensor("constv", [128, 4], F32, kind="ExternalInput").ap()
    d_onesk = nc.dram_tensor("onesk", [1, 128], F32, kind="ExternalInput").ap()
    d_onesp = nc.dram_tensor("onesp", [128, 1], F32, kind="ExternalInput").ap()
    d_out = nc.dram_tensor("out", [T, D], F32, kind="ExternalOutput").ap()
    d_tgT = nc.dram_tensor("tgT", [D, DI], BF16, kind="Internal").ap()
    d_tuT = nc.dram_tensor("tuT", [D, DI], BF16, kind="Internal").ap()
    d_tdT = nc.dram_tensor("tdT", [DI, D], BF16, kind="Internal").ap()

    x_r = d_x.rearrange("(c p) d -> p c d", p=128)
    out_r = d_out.rearrange("(c p) d -> p c d", p=128)

    with ExitStack() as es:
        S = lambda n, sh, dt: es.enter_context(nc.sbuf_tensor(n, sh, dt))
        sch = Sched(nc, es)
        ST = sch.token

        s_x1 = S("s_x1", [128, NC, D], BF16)
        s_lnf = S("s_lnf", [128, NC, D], BF16)
        s_dyq = S("s_dyq", [128, KC, T], BF16)
        s_bd = S("s_bd", [128, KC], F32)
        s_nega = S("s_nega", [128, KC, NSTATE], F32)
        s_bprm = S("s_bprm", [128, KC, NSTATE], F32)
        s_borep = S("s_borep", [128, D], BF16)
        s_ident = S("s_ident", [128, 128], BF16)
        s_onesk = S("s_onesk", [1, 128], F32)
        s_onesp = S("s_onesp", [128, 1], F32)
        s_const = S("s_const", [128, 4], F32)
        s_wsc = S("s_wsc", [128, 6], F32)
        s_r1g = S("s_r1g", [128, NC], F32)
        s_r1u = S("s_r1u", [128, NC], F32)
        s_r2d = S("s_r2d", [128, NC], F32)
        s_stats = S("s_stats", [128, 8], F32)
        s_qs = S("s_qs", [128, 4], F32)
        QT = T // 4
        s_abar = S("s_abar", [128, 4, QT], F32)
        s_sig = S("s_sig", [128, 2, QT], BF16)
        s_ytmp = S("s_ytmp", [128, QT], BF16)
        s_state = S("s_state", [128, KC * NSTATE], BF16)
        sC_st = S("sC_st", [128, 8], F32)
        sC_nt = S("sC_nt", [128, 16], F32)
        es_w_slab = ExitStack()
        s_w = es_w_slab.enter_context(nc.sbuf_tensor("s_w", [128, KC, D], BF16))

        lnT_view = s_lnf[:].rearrange("p a b -> p (a b)").rearrange("p (k t) -> p k t", k=KC)
        delta_view = s_dyq[:]

        sch.dma("sync", "consts", lambda h: h.dma_start(out=s_bd[:], in_=d_bd))
        sch.dma("sync", "consts", lambda h: h.dma_start(out=s_bprm[:], in_=d_bprm))
        sch.dma("gpsimd", "borep", lambda h: h.dma_start(out=s_borep[:], in_=d_borep))
        sch.dma("sync", "consts", lambda h: h.dma_start(out=s_ident[:], in_=d_ident))
        sch.dma("sync", "consts", lambda h: h.dma_start(out=s_onesk[:], in_=d_onesk))
        sch.dma("sync", "consts", lambda h: h.dma_start(out=s_onesp[:], in_=d_onesp))
        sch.dma("sync", "consts", lambda h: h.dma_start(
            out=s_nega[:].rearrange("p k n -> p (k n)"), in_=d_alog))
        t_x0 = sch.dma("gpsimd", "x0", lambda h: h.dma_start(out=s_x1[:], in_=x_r))
        t_consts = sch.dma("sync", "consts", lambda h: h.dma_start(out=s_const[:], in_=d_const))
        t_nega_ld = t_consts
        t_magic = t_consts
        s_magic_ap = s_const[:, 0:1]
        s_eps_ap = s_const[:, 1:2]
        s_negh_ap = s_const[:, 2:3]
        s_negm_ap = s_const[:, 3:4]

        with nc.sbuf_tensor("aw1", [128, KC * NSTATE], F32) as aw1, \
             nc.sbuf_tensor("aw2", [128, KC * NSTATE], F32) as aw2:
            av = s_nega[:].rearrange("p k n -> p (k n)")
            sch.add("scalar", lambda h: h.activation(out=aw1[:], in_=av, func=AF.Abs),
                    waits=[t_nega_ld])
            sch.add("scalar", lambda h: h.activation(out=aw1[:], in_=aw1[:], func=AF.Exp, scale=-1.0),
                    waits=[sch.chain("scalar")])
            sch.add("scalar", lambda h: h.activation(out=aw1[:], in_=aw1[:], func=AF.Ln, bias=1.0),
                    waits=[sch.chain("scalar")])
            sch.add("scalar", lambda h: h.activation(out=aw2[:], in_=av, func=AF.Relu), inc=1)
            sch.add("vector", lambda h: h.tensor_tensor(aw1[:], aw1[:], aw2[:], ALU.add),
                    waits=[ST("scalar")])
            sch.add("vector", lambda h: h.tensor_scalar(av, aw1[:], 1e-4, -1.0, ALU.add, ALU.mult),
                    waits=[sch.chain("vector")])
            sch.barrier()

        def _finish():
            block = es.enter_context(nc.Block())
            sch.emit(block)
            return nc

        if stop_phase <= 0:
            return _finish()

        def prep_square(d_wsrc, name, barrier=True, dma_waits=(), first_tr_waits=()):
            with nc.sbuf_tensor(f"{name}_nat", [128, KC, D], BF16) as wnat, \
                 nc.psum_tensor(f"{name}_pt", [128, KC, 1024], BF16) as pt:
                t_w = sch.dma("gpsimd", f"{name}_nat",
                              lambda h: h.dma_start(out=wnat[:],
                                                    in_=d_wsrc.rearrange("(ec p) d -> p ec d", p=128)),
                              waits=list(dma_waits))
                for dc in range(KC):
                    for ec in range(KC):
                        w = []
                        if dc == 0 and ec == 0:
                            w.append(t_w)
                            w += list(first_tr_waits)
                        if dc > 0 and ec == 0:
                            w.append(ST("scalar"))
                        sch.add("tensor",
                                lambda h, a=ec, b=dc: h.transpose(
                                    pt[:, a, 0:128], wnat[:, a, b * 128:(b + 1) * 128], s_ident[:]),
                                waits=w, inc=(1 if ec == KC - 1 else 0))
                    sch.add("scalar",
                            lambda h, b=dc: h.activation(
                                out=s_w[:, b, :].rearrange("p (k q) -> p k q", q=128),
                                in_=pt[:, :, 0:128], func=AF.Copy),
                            waits=[ST("tensor")], inc=1)
                if barrier:
                    sch.barrier()

        prep_square(d_wd, "wd")
        if stop_phase <= 1:
            return _finish()

        scan_done = {}
        pend_saves = []

        def flush_save(force=False):
            while pend_saves and (force or len(pend_saves) > 1):
                c_s, p_s, tok = pend_saves.pop(0)
                sch.add("gpsimd",
                        lambda h, c_=c_s, p=p_s: h.tensor_tensor(
                            s_state[:, c_:c_ + 1], s_sig[:, p, QT - 1:QT], s_onesp[:],
                            ALU.mult),
                        waits=[tok, sch.chain("gpsimd")])
            return ST("gpsimd")

        def emit_B_dc(q, dc, inject=None, w_extra=()):
            t0 = q * QT
            for n in range(NSTATE):
                pa = n % 4
                ps = n % 2
                col = dc * NSTATE + n
                if n >= 4:
                    key = (dc, n - 4)
                else:
                    key = ((dc - 1) % KC, NSTATE - 4 + n)
                w_buf = [scan_done[key]] if key in scan_done else []
                t_pool = flush_save()
                sch.add("scalar",
                        lambda h, d=dc, n_=n, p=pa: h.activation(
                            out=s_abar[:, p, :], in_=delta_view[:, d, t0:t0 + QT],
                            func=AF.Exp, scale=s_nega[:, d, n_:n_ + 1]),
                        waits=w_buf + (list(w_extra) if n == 0 else []) + [sch.chain("scalar")])
                t_exp = ST("scalar")
                if q > 0:
                    scan_done[(dc, n)] = sch.add("vector",
                            lambda h, d=dc, c_=col, p=pa, s=ps: h.tensor_tensor_scan(
                                s_sig[:, s, :], s_abar[:, p, :],
                                lnT_view[:, d, t0:t0 + QT], s_state[:, c_:c_ + 1],
                                ALU.mult, ALU.add),
                            waits=[t_exp, t_pool, sch.chain("vector")])
                else:
                    scan_done[(dc, n)] = sch.add("vector",
                            lambda h, d=dc, p=pa, s=ps: h.tensor_tensor_scan(
                                s_sig[:, s, :], s_abar[:, p, :],
                                lnT_view[:, d, t0:t0 + QT], 0.0,
                                ALU.mult, ALU.add),
                            waits=[t_exp, t_pool, sch.chain("vector")])
                if q < 3:
                    pend_saves.append((col, ps, scan_done[(dc, n)]))
                if n == 0:
                    sch.add("vector", lambda h, d=dc, n_=n, s=ps: h.tensor_scalar(
                        s_ytmp[:], s_sig[:, s, :], s_bprm[:, d, n_:n_ + 1], None, ALU.mult),
                        waits=[sch.chain("vector")])
                elif n == NSTATE - 1:
                    sch.add("vector", lambda h, d=dc, n_=n, s=ps: h.scalar_tensor_tensor(
                        delta_view[:, d, t0:t0 + QT], s_sig[:, s, :],
                        s_bprm[:, d, n_:n_ + 1], s_ytmp[:], ALU.mult, ALU.add),
                        waits=[sch.chain("vector")])
                else:
                    sch.add("vector", lambda h, d=dc, n_=n, s=ps: h.scalar_tensor_tensor(
                        s_ytmp[:], s_sig[:, s, :], s_bprm[:, d, n_:n_ + 1], s_ytmp[:],
                        ALU.mult, ALU.add),
                        waits=[sch.chain("vector")])
                if inject is not None:
                    inject()


        def _prep_geom(d_wsrc, O, I):
            RC = O // 128
            CC = I // 1024
            IC = I // 128
            _wv = d_wsrc.rearrange("(b p) (c i) -> p b c i", p=128, i=1024)
            if CC == 1:
                wv4_slice = lambda q: _wv[:, 2 * q:2 * q + 2, 0, :]
            else:
                wv4_slice = lambda q: _wv[:, q // 2, 2 * (q % 2):2 * (q % 2) + 2, :]
            return RC, CC, IC, wv4_slice

        def prep_t_pass1(d_wsrc, d_wT, O, I, wsc_col, name, es_w):
            RC, CC, IC, wv4_slice = _prep_geom(d_wsrc, O, I)
            Sx = lambda n, sh, dt: es_w.enter_context(nc.sbuf_tensor(n, sh, dt))
            RING = 4 if CC == 1 else 2
            wbuf = Sx(f"{name}_b1", [128, RING, 2, 1024], F32)
            acc = Sx(f"{name}_acc", [128, 2], F32)
            dum = Sx(f"{name}_dum", [128, 2048], BF16)
            sc2 = Sx(f"{name}_sc2", [1, 2], F32)
            sch.add("gpsimd", lambda h: h.memset(acc[:, 0:1], 0.0),
                    waits=[ST("scalar"), sch.chain("gpsimd")])
            t_abs = {}
            for q in range(RC * CC // 2):
                par = q % RING
                if q >= RING:
                    w_dma = [t_abs[q - RING]]
                else:
                    w_dma = [ST("gpsimd"), ST("scalar")]
                sch.dma("sync", f"wbuf{par}",
                        lambda h, q_=q, p=par: h.dma_start(
                            out=wbuf[:, p, :, :], in_=wv4_slice(q_)),
                        waits=w_dma)
                t_d = (f"dma_wbuf{par}", sch.counts[f"dma_wbuf{par}"])
                sch.add("scalar",
                        lambda h, p=par: h.activation(
                            out=dum[:],
                            in_=wbuf[:, p, :, :].rearrange("p a b -> p (a b)"),
                            func=AF.Abs, accum_out=acc[:, 1:2]),
                        waits=[t_d, ST("gpsimd"), sch.chain("scalar")])
                t_abs[q] = ST("scalar")
                sch.add("gpsimd", lambda h: h.tensor_tensor(acc[:, 0:1], acc[:, 0:1],
                                                            acc[:, 1:2], ALU.add),
                        waits=[ST("scalar"), sch.chain("gpsimd")])
                yield
            with nc.psum_tensor(f"{name}_p1", [1, 1], F32) as p1, \
                 nc.psum_tensor(f"{name}_p2", [128, 2], F32) as p2:
                sch.pe_fence()
                sch.add("tensor", lambda h: h.matmul(p1[:], acc[:, 0:1], s_onesp[:],
                                                     start=True, stop=True),
                        waits=[ST("gpsimd")], inc=1)
                sch.add("scalar", lambda h: h.copy(out=sc2[:, 0:1], in_=p1[:]),
                        waits=[ST("tensor"), sch.chain("scalar")])
                sch.add("vector", lambda h: h.tensor_scalar(sc2[:, 0:1], sc2[:, 0:1],
                                                            1.0 / (O * I), 1e-5,
                                                            ALU.mult, ALU.max),
                        waits=[ST("scalar"), sch.chain("vector")])
                sch.add("vector", lambda h: h.reciprocal(sc2[:, 1:2], sc2[:, 0:1]),
                        waits=[sch.chain("vector")])
                sch.add("tensor", lambda h: h.matmul(p2[:], s_onesk[:], sc2[:],
                                                     start=True, stop=True),
                        waits=[ST("vector")], inc=1)
                sch.add("scalar", lambda h, wc=wsc_col: h.activation(
                    out=s_wsc[:, wc:wc + 2], in_=p2[:], func=AF.Copy),
                    waits=[ST("tensor"), sch.chain("scalar")])
                sch.add("tensor", None, waits=[ST("scalar")])
                sch.pe_fence()
            yield

        def prep_t_pass2(d_wsrc, d_wT, O, I, wsc_col, name, es_w):
            RC, CC, IC, wv4_slice = _prep_geom(d_wsrc, O, I)
            wTv = d_wT.rearrange("(ic p) o -> p ic o", p=128)
            Sx = lambda n, sh, dt: es_w.enter_context(nc.sbuf_tensor(n, sh, dt))
            RING = 4 if CC == 1 else 2
            wbuf = Sx(f"{name}_b2", [128, RING, 2, 1024], F32)
            tern = Sx(f"{name}_tern", [128, 2, I], BF16)
            stage = Sx(f"{name}_stage", [128, IC, 256], BF16)
            swcol = s_wsc[:, wsc_col + 1:wsc_col + 2]
            stage_v = stage[:].rearrange("p ic (two q) -> p ic two q", two=2)
            with nc.psum_tensor(f"{name}_pta", [128, 2, 1024], BF16) as pta, \
                 nc.psum_tensor(f"{name}_ptb", [128, 2, 1024], BF16) as ptb:
                pts4 = (pta, ptb)
                TPD = max(2 // CC, 1)
                bi = 0
                t_pe_hist = {}
                evac_args = {}
                evac_tok = {}

                stage_guard = [None]

                def emit_evac(k):
                    half_, bb_ = evac_args[k]
                    w = [t_pe_hist[k], sch.chain("scalar")]
                    if stage_guard[0] is not None:
                        w.append(stage_guard[0])
                        stage_guard[0] = None
                    sch.add("scalar",
                            (lambda p, b, pp: lambda h: h.activation(
                                out=stage_v[:, b * 2:(b + 1) * 2, p, :],
                                in_=pts4[pp][:, :, 0:128], func=AF.Copy))(half_, bb_, k % 2),
                            waits=w)
                    evac_tok[k] = ST("scalar")

                t_sign = {}
                for rp in range(RC // 2):
                    for half in range(2):
                        r = rp * 2 + half
                        for c in range(CC):
                            gt = r * CC + c
                            if gt % 2 == 0:
                                q = gt // 2
                                par4 = q % RING
                                sch.dma("sync", f"wbuf{par4}",
                                        lambda h, q_=q, p=par4: h.dma_start(
                                            out=wbuf[:, p, :, :], in_=wv4_slice(q_)),
                                        waits=[t_sign[q - RING] if q >= RING
                                               else ST("scalar")])
                            q = gt // 2
                            par4 = q % RING
                            tl = gt % 2
                            t_d = (f"dma_wbuf{par4}", sch.counts[f"dma_wbuf{par4}"])
                            sch.add("scalar",
                                    lambda h, p=par4, t_=tl: h.activation(
                                        out=wbuf[:, p, t_, :], in_=wbuf[:, p, t_, :],
                                        func=AF.Identity, scale=swcol, bias=s_magic_ap),
                                    waits=[t_d, t_magic, ST("tensor"), sch.chain("scalar")])
                            sch.add("scalar",
                                    lambda h, p=half, c_=c, pp=par4, t_=tl: h.activation(
                                        out=tern[:, p, c_ * 1024:(c_ + 1) * 1024],
                                        in_=wbuf[:, pp, t_, :], func=AF.Sign, bias=s_negm_ap),
                                    waits=[ST("tensor"), sch.chain("scalar")])
                            if tl == 1:
                                t_sign[q] = ST("scalar")
                            yield
                        ternv = tern[:].rearrange("p b (ic q) -> p b ic q", q=128)
                        t_q = ST("scalar")
                        for bb in range(IC // 2):
                            ppar = bi % 2
                            for j in range(2):
                                w = []
                                if j == 0:
                                    w.append(t_q)
                                    if bi >= 2:
                                        w.append(evac_tok[bi - 2])
                                sch.add("tensor",
                                        lambda h, p=half, i=bb * 2 + j, j_=j, pp=ppar: h.transpose(
                                            pts4[pp][:, j_, 0:128], ternv[:, p, i, :], s_ident[:]),
                                        waits=w, inc=1)
                            t_pe_hist[bi] = ST("tensor")
                            evac_args[bi] = (half, bb)
                            if bi >= 1 and (bi - 1) not in evac_tok:
                                emit_evac(bi - 1)
                            bi += 1
                            yield
                    if (bi - 1) not in evac_tok:
                        emit_evac(bi - 1)
                    t_ev = ST("scalar")
                    t_st2 = sch.dma("gpsimd", "stage_st",
                                    lambda h, rp_=rp: h.dma_start(
                                        out=wTv[:, :, rp_ * 256:(rp_ + 1) * 256], in_=stage[:]),
                                    waits=[t_ev])
                    stage_guard[0] = t_st2
                    yield
            sch.add("scalar", None, waits=[stage_guard[0]])
            stage_guard[0] = None
            sch.pe_fence()

        def make_prep_driver(genfn, queue):
            import contextlib
            state = {"stack": None, "gen": None, "queue": list(queue)}

            def step():
                while True:
                    if state["gen"] is None:
                        if not state["queue"]:
                            return False
                        args = state["queue"].pop(0)
                        state["stack"] = contextlib.ExitStack()
                        state["gen"] = genfn(*args, state["stack"])
                    try:
                        next(state["gen"])
                        return True
                    except StopIteration:
                        state["stack"].close()
                        state["gen"] = None
                        state["stack"] = None

            return step

        prep_p1 = make_prep_driver(prep_t_pass1,
                                   [(d_wg, d_tgT, DI, D, 0, "wg"),
                                    (d_wu, d_tuT, DI, D, 2, "wu"),
                                    (d_wdn, d_tdT, D, DI, 4, "wdn")])
        prep_step = make_prep_driver(prep_t_pass2,
                                     [(d_wg, d_tgT, DI, D, 0, "wg"),
                                      (d_wu, d_tuT, DI, D, 2, "wu")])
        prep_step2 = make_prep_driver(prep_t_pass2,
                                      [(d_wdn, d_tdT, D, DI, 4, "wdn")])


        def emit_newton4(st4, nt, t_sums):
            sch.add("vector", lambda h: h.tensor_scalar(
                nt[:, 0:4], st4[:, 0:4], 1.0 / D, None, ALU.mult),
                waits=[t_sums, sch.chain("vector")])
            sch.add("vector", lambda h: h.tensor_tensor(
                nt[:, 4:8], st4[:, 0:4], nt[:, 0:4], ALU.mult),
                waits=[sch.chain("vector")])
            sch.add("vector", lambda h: h.tensor_tensor(
                nt[:, 4:8], st4[:, 4:8], nt[:, 4:8], ALU.subtract),
                waits=[sch.chain("vector")])
            sch.add("vector", lambda h: h.tensor_scalar(
                nt[:, 4:8], nt[:, 4:8], 1.0 / D, EPS, ALU.mult, ALU.add),
                waits=[sch.chain("vector")])
            iv = nt[:, 4:8].bitcast(I32)
            iy = nt[:, 8:12].bitcast(I32)
            sch.add("vector", lambda h: h.tensor_scalar(
                iy, iv, 1, None, ALU.logical_shift_right),
                waits=[sch.chain("vector")])
            sch.add("vector", lambda h: h.tensor_scalar(
                iy, iy, -1, 0x5F3759DF, ALU.mult, ALU.add),
                waits=[sch.chain("vector")])
            for _ in range(3):
                sch.add("vector", lambda h: h.tensor_tensor(
                    nt[:, 12:16], nt[:, 8:12], nt[:, 8:12], ALU.mult),
                    waits=[sch.chain("vector")])
                sch.add("vector", lambda h: h.tensor_tensor(
                    nt[:, 12:16], nt[:, 12:16], nt[:, 4:8], ALU.mult),
                    waits=[sch.chain("vector")])
                sch.add("vector", lambda h: h.tensor_scalar(
                    nt[:, 12:16], nt[:, 12:16], -0.5, 1.5, ALU.mult, ALU.add),
                    waits=[sch.chain("vector")])
                sch.add("vector", lambda h: h.tensor_tensor(
                    nt[:, 8:12], nt[:, 8:12], nt[:, 12:16], ALU.mult),
                    waits=[sch.chain("vector")])
            sch.add("vector", lambda h: h.tensor_tensor(
                nt[:, 0:4], nt[:, 0:4], nt[:, 8:12], ALU.mult),
                waits=[sch.chain("vector")])
            sch.add("vector", lambda h: h.tensor_scalar(
                nt[:, 0:4], nt[:, 0:4], -1.0, None, ALU.mult),
                waits=[sch.chain("vector")])
            return ST("vector")

        def emit_lnstats(src_ap, dummy, waits=()):
            sch.add("scalar", lambda h: h.activation(out=dummy[:], in_=src_ap, func=AF.Identity,
                                                     accum_out=s_stats[:, 0:1]), waits=list(waits))
            sch.add("scalar", lambda h: h.activation(out=dummy[:], in_=src_ap, func=AF.Square,
                                                     accum_out=s_stats[:, 1:2]),
                    waits=[sch.chain("scalar")])
            t_s = ST("scalar")
            sch.add("vector", lambda h: h.tensor_scalar(s_stats[:, 2:3], s_stats[:, 0:1],
                                                        1.0 / D, None, ALU.mult),
                    waits=[t_s, sch.chain("vector")])
            sch.add("vector", lambda h: h.scalar_tensor_tensor(s_stats[:, 3:4], s_stats[:, 0:1],
                                                               s_stats[:, 2:3], s_stats[:, 1:2],
                                                               ALU.mult, ALU.subtract),
                    waits=[sch.chain("vector")])
            sch.add("vector", lambda h: h.tensor_scalar(s_stats[:, 3:4], s_stats[:, 3:4],
                                                        -1.0 / D, None, ALU.mult),
                    waits=[sch.chain("vector")])
            t_v = ST("vector")
            sch.add("scalar", lambda h: h.activation(out=s_stats[:, 6:7], in_=s_stats[:, 3:4],
                                                     func=AF.Ln, bias=s_eps_ap), waits=[t_v])
            sch.add("scalar", lambda h: h.activation(out=s_stats[:, 4:5], in_=s_stats[:, 6:7],
                                                     func=AF.Exp, scale=s_negh_ap),
                    waits=[sch.chain("scalar")], inc=1)
            t_r = ST("scalar")
            sch.add("vector", lambda h: h.scalar_tensor_tensor(s_stats[:, 5:6], s_stats[:, 2:3],
                                                               -1.0, s_stats[:, 4:5],
                                                               ALU.mult, ALU.mult),
                    waits=[t_r, sch.chain("vector")])
            return ST("vector")

        with nc.sbuf_tensor("lnscA", [128, D], F32) as lnscA, \
             nc.sbuf_tensor("lnbfA", [128, D], BF16) as lnbfA, \
             nc.sbuf_tensor("spA0", [128, 512], F32) as spA0, \
             nc.sbuf_tensor("spA1", [128, 512], F32) as spA1:
            with nc.psum_tensor("ptA", [128, KC, 1024], BF16) as ptA:
                for tc in range(NC):
                    x0c = s_x1[:, tc, :]
                    t_st = emit_lnstats(x0c, lnscA, waits=[t_x0] if tc == 0 else ())
                    sch.add("scalar",
                            lambda h, src=x0c: h.activation(out=lnbfA[:], in_=src, func=AF.Identity,
                                                            scale=s_stats[:, 4:5], bias=s_stats[:, 5:6]),
                            waits=[t_st], inc=1)
                    t_ln = ST("scalar")
                    for ic in range(KC):
                        w = []
                        if ic == 0:
                            w.append(t_ln)
                            if tc >= 1:
                                w.append(ST("scalar"))
                        sch.add("tensor",
                                lambda h, i=ic: h.transpose(
                                    ptA[:, i, 0:128], lnbfA[:, i * 128:(i + 1) * 128], s_ident[:]),
                                waits=w, inc=(1 if ic == KC - 1 else 0))
                    t_pe = ST("tensor")
                    sch.add("scalar",
                            lambda h, t_=tc: h.activation(
                                out=lnT_view[:, :, t_ * 128:(t_ + 1) * 128],
                                in_=ptA[:, :, 0:128], func=AF.Copy),
                            waits=[t_pe], inc=1)
            if stop_phase <= 1.5:
                return _finish()
            sch.pe_fence()

            with nc.psum_tensor("pdA0", [128, 512], F32) as pdA0, \
                 nc.psum_tensor("pdA1", [128, 512], F32) as pdA1:
                psd = (pdA0, pdA1)
                bq0 = iter(range(KC))
                t_ux0 = None
                t_soft = {}
                for g in range(4):
                    for ec in range(KC):
                        if g >= 1 and ec % 3 == 0:
                            dcb = next(bq0, None)
                            if dcb is not None:
                                emit_B_dc(0, dcb, w_extra=[t_ux0])
                        prep_p1()
                        prep_p1()
                        it = g * KC + ec
                        par = it % 2
                        for kc in range(KC):
                            w = []
                            if kc == 0:
                                w.append(t_soft.get(it - 2, ST("scalar")))
                            sch.add("tensor",
                                    lambda h, p=par, e=ec, k=kc, g_=g: h.matmul(
                                        psd[p][:], s_w[:, k, e * 128:(e + 1) * 128],
                                        lnT_view[:, k, g_ * 512:(g_ + 1) * 512],
                                        start=(k == 0), stop=(k == KC - 1)),
                                    waits=w, inc=(1 if kc == KC - 1 else 0))
                        t_mm = ST("tensor")
                        sch.add("scalar",
                                lambda h, p=par, e=ec: h.activation(out=spA0[:], in_=psd[p][:],
                                                                    func=AF.Abs, bias=s_bd[:, e:e + 1]),
                                waits=[t_mm])
                        sch.add("scalar", lambda h: h.activation(out=spA0[:], in_=spA0[:],
                                                                 func=AF.Exp, scale=-1.0),
                                waits=[sch.chain("scalar")])
                        sch.add("scalar", lambda h: h.activation(out=spA0[:], in_=spA0[:],
                                                                 func=AF.Ln, bias=1.0),
                                waits=[sch.chain("scalar")])
                        sch.add("scalar",
                                lambda h, p=par, e=ec: h.activation(out=spA1[:], in_=psd[p][:],
                                                                    func=AF.Relu, bias=s_bd[:, e:e + 1]),
                                inc=1)
                        t_sp = ST("scalar")
                        t_soft[it] = t_sp
                        sch.add("vector",
                                lambda h, e=ec, g_=g: h.tensor_tensor(
                                    delta_view[:, e, g_ * 512:(g_ + 1) * 512], spA0[:], spA1[:], ALU.add),
                                waits=[t_sp, sch.chain("vector")])
                        sch.add("scalar", None, waits=[ST("vector")])
                    t_mmw = ST("tensor")
                    for d in range(KC):
                        sch.add("vector", lambda h, d_=d, g_=g: h.tensor_tensor(
                            lnT_view[:, d_, g_ * QT:(g_ + 1) * QT],
                            lnT_view[:, d_, g_ * QT:(g_ + 1) * QT],
                            delta_view[:, d_, g_ * QT:(g_ + 1) * QT], ALU.mult),
                            waits=([t_mmw] if d == 0 else []) + [sch.chain("vector")])
                    if g == 0:
                        t_ux0 = ST("vector")
                for dcb in bq0:
                    emit_B_dc(0, dcb, w_extra=[t_ux0])
                while prep_p1():
                    pass

        if stop_phase <= 2:
            return _finish()
        sch.pe_fence()
        prep_square(d_wo, "wo", barrier=False,
                    dma_waits=[ST("scalar"), ST("vector")],
                    first_tr_waits=[ST("scalar")])
        t_wo = ST("scalar")

        flush_save(force=True)
        t_y0 = ST("vector")
        bq1 = iter(range(KC))
        while prep_step():
            dc = next(bq1, None)
            if dc is not None:
                emit_B_dc(1, dc)
        t_prep_done = ST("scalar")

        if stop_phase <= 5:
            return _finish()
        def emit_C_q(q, w_first=(), inject=None):
            tcs = range(4 * q, 4 * q + 4)
            with nc.sbuf_tensor(f"attnC{q}", [128, 4, D], BF16) as attnC4, \
                 nc.sbuf_tensor(f"lnscC{q}", [128, D], F32) as lnscC, \
                 nc.sbuf_tensor(f"q1C{q}", [128, D], BF16) as q1C:
                ln3C = attnC4[:, 0:2, :].bitcast(F32).rearrange("p a b -> p (a b)")
                with nc.psum_tensor(f"pW0{q}", [128, 512], F32) as pW0, \
                     nc.psum_tensor(f"pW1{q}", [128, 512], F32) as pW1:
                    for ci, tc in enumerate(tcs):
                        for half, pw in ((0, pW0), (1, pW1)):
                            for kc in range(KC):
                                w = []
                                if kc == 0 and half == 0:
                                    if tc == 4 * q:
                                        w += list(w_first)
                                    else:
                                        w.append(ST("vector"))
                                sch.add("tensor",
                                        lambda h, k=kc, hf=half, t_=tc, p=pw: h.matmul(
                                            p[:], delta_view[:, k, t_ * 128:(t_ + 1) * 128],
                                            s_w[:, k, hf * 512:(hf + 1) * 512],
                                            start=(k == 0), stop=(k == KC - 1)),
                                        waits=w, inc=(1 if (half, kc) == (1, KC - 1) else 0))
                        t_mm = ST("tensor")
                        sch.add("vector", lambda h, c_=ci: h.tensor_tensor(
                            attnC4[:, c_, 0:512], pW0[:], s_borep[:, 0:512], ALU.add),
                            waits=[t_mm, ("dma_borep", 16), sch.chain("vector")])
                        sch.add("vector", lambda h, c_=ci: h.tensor_tensor(
                            attnC4[:, c_, 512:1024], pW1[:], s_borep[:, 512:1024], ALU.add),
                            waits=[sch.chain("vector")])
                        t_attn = ST("vector")
                        sch.add("scalar", lambda h, c_=ci: h.activation(
                            out=lnscC[:], in_=attnC4[:, c_, :], func=AF.Identity,
                            accum_out=sC_st[:, c_:c_ + 1]),
                            waits=[t_attn, sch.chain("scalar")])
                        sch.add("scalar", lambda h, c_=ci: h.activation(
                            out=lnscC[:], in_=attnC4[:, c_, :], func=AF.Square,
                            accum_out=sC_st[:, 4 + c_:5 + c_]),
                            waits=[sch.chain("scalar")])
                    t_rs2 = emit_newton4(sC_st, sC_nt, ST("scalar"))
                    for ci, tc in enumerate(tcs):
                        sch.add("scalar", lambda h, c_=ci: h.activation(
                            out=lnscC[:], in_=attnC4[:, c_, :], func=AF.Identity,
                            scale=sC_nt[:, 8 + c_:9 + c_], bias=sC_nt[:, c_:c_ + 1]),
                            waits=([t_rs2] if ci == 0 else []) + [sch.chain("scalar")])
                        t_ln2 = ST("scalar")
                        sch.add("vector", lambda h, t_=tc: h.tensor_tensor(
                            s_x1[:, t_, :], s_x1[:, t_, :], lnscC[:], ALU.add),
                            waits=[t_ln2, sch.chain("vector")])
                        sch.add("scalar", None, waits=[ST("vector")])
                sch.pe_fence()
                with nc.psum_tensor(f"ptC{q}", [128, 4, 1024], BF16) as ptC:
                    for ci, tc in enumerate(tcs):
                        sch.add("scalar", lambda h, t_=tc, c_=ci: h.activation(
                            out=lnscC[:], in_=s_x1[:, t_, :], func=AF.Identity,
                            accum_out=sC_st[:, c_:c_ + 1]),
                            waits=[sch.chain("scalar")])
                        sch.add("scalar", lambda h, t_=tc, c_=ci: h.activation(
                            out=lnscC[:], in_=s_x1[:, t_, :], func=AF.Square,
                            accum_out=sC_st[:, 4 + c_:5 + c_]),
                            waits=[sch.chain("scalar")])
                    t_rs3 = emit_newton4(sC_st, sC_nt, ST("scalar"))
                    for ci, tc in enumerate(tcs):
                        sch.add("scalar",
                                lambda h, t_=tc, c_=ci: h.activation(
                                    out=ln3C, in_=s_x1[:, t_, :], func=AF.Identity,
                                    scale=sC_nt[:, 8 + c_:9 + c_], bias=sC_nt[:, c_:c_ + 1]),
                                waits=([t_rs3] if ci == 0 else []) + [sch.chain("scalar")])
                        t_ln3 = ST("scalar")
                        sch.add("vector", lambda h: h.tensor_reduce(
                            s_qs[:, 0:1], ln3C[:], AX, ALU.max, apply_absolute_value=True),
                            waits=[t_ln3, sch.chain("vector")])
                        sch.add("vector", lambda h: h.tensor_scalar(
                            s_qs[:, 0:1], s_qs[:, 0:1], 1e-5, None, ALU.max),
                            waits=[sch.chain("vector")])
                        sch.add("vector", lambda h: h.reciprocal(s_qs[:, 1:2], s_qs[:, 0:1]),
                                waits=[sch.chain("vector")])
                        sch.add("vector", lambda h: h.tensor_scalar(
                            s_qs[:, 1:2], s_qs[:, 1:2], 127.0, None, ALU.mult),
                            waits=[sch.chain("vector")])
                        sch.add("vector", lambda h, t_=tc: h.tensor_scalar(
                            s_r1g[:, t_:t_ + 1], s_qs[:, 0:1], 1.0 / 127.0, None, ALU.mult),
                            waits=[sch.chain("vector")])
                        t_q = ST("vector")
                        sch.add("scalar", lambda h: h.activation(out=lnscC[:], in_=ln3C[:],
                                                                 func=AF.Identity,
                                                                 scale=s_qs[:, 1:2],
                                                                 bias=s_magic_ap),
                                waits=[t_q], inc=1)
                        sch.add("vector", lambda h: h.tensor_scalar(
                            q1C[:], lnscC[:], MAGIC, None, ALU.subtract),
                            waits=[ST("scalar"), sch.chain("vector")])
                        t_q1 = ST("vector")
                        for bb in range(2):
                            for j in range(4):
                                ic = bb * 4 + j
                                w = []
                                if j == 0:
                                    w.append(t_q1)
                                    w.append(ST("scalar"))
                                sch.add("tensor",
                                        lambda h, i=ic, j_=j: h.transpose(
                                            ptC[:, j_, 0:128], q1C[:, i * 128:(i + 1) * 128],
                                            s_ident[:]),
                                        waits=w, inc=(1 if j == 3 else 0))
                            t_tr = ST("tensor")
                            sch.add("scalar",
                                    lambda h, t_=tc, b=bb: h.activation(
                                        out=delta_view[:, b * 4:(b + 1) * 4,
                                                       t_ * 128:(t_ + 1) * 128],
                                        in_=ptC[:, :, 0:128], func=AF.Copy),
                                    waits=[t_tr, sch.chain("scalar")])
                        if inject is not None:
                            for _ in range(6):
                                inject()
                    if inject is not None:
                        while inject():
                            pass
            sch.pe_fence()
            c0 = 4 * q
            sch.add("vector", lambda h: h.tensor_scalar(
                s_r1u[:, c0:c0 + 4], s_r1g[:, c0:c0 + 4], s_wsc[:, 2:3], None, ALU.mult),
                waits=[sch.chain("vector")])
            sch.add("vector", lambda h: h.tensor_scalar(
                s_r1g[:, c0:c0 + 4], s_r1g[:, c0:c0 + 4], s_wsc[:, 0:1], None, ALU.mult),
                waits=[sch.chain("vector")])
            return ST("scalar")

        def c0_inject():
            r = prep_step2()
            dc = next(bq1, None)
            if dc is not None:
                emit_B_dc(1, dc)
            return r

        t_c_done = emit_C_q(0, w_first=[t_y0, t_prep_done, t_wo], inject=c0_inject)
        for dc in bq1:
            emit_B_dc(1, dc)
        flush_save(force=True)
        t_y_q = {1: ST("vector")}
        bpend = [(2, dc) for dc in range(KC)] + [(3, dc) for dc in range(KC)]

        def pop_B(k):
            while k > 0 and bpend:
                qq, dc = bpend.pop(0)
                emit_B_dc(qq, dc)
                if dc == KC - 1:
                    flush_save(force=True)
                    t_y_q[qq] = ST("vector")
                k -= 1
        sch.barrier()

        if stop_phase <= 6:
            return _finish()
        q1T_view = s_dyq[:]
        f_ap = lambda tc: s_dyq[:, 2 * (tc % 4):2 * (tc % 4) + 2,
                                (tc // 4) * 512:(tc // 4 + 1) * 512]
        tgT_v = d_tgT.rearrange("(kc p) o -> p kc o", p=128)
        tuT_v = d_tuT.rearrange("(kc p) o -> p kc o", p=128)
        tdT_v = d_tdT.rearrange("(kc p) o -> p kc o", p=128).rearrange(
            "p (q f) o -> p q f o", f=2)
        with nc.sbuf_tensor("s_h", [128, 4, DI], BF16) as s_h, \
             nc.sbuf_tensor("s_tg", [128, 2, 4, 512], BF16) as s_tg, \
             nc.sbuf_tensor("s_tu", [128, 2, 4, 512], BF16) as s_tu, \
             nc.sbuf_tensor("s_gtmp", [128, 2, 512], BF16) as s_gtmp, \
             nc.sbuf_tensor("s_q2sc", [128, 4], F32) as s_q2sc, \
             nc.sbuf_tensor("s_q2s", [128, 512], F32) as s_q2s, \
             nc.sbuf_tensor("s_q2b", [128, 512], BF16) as s_q2b, \
             nc.sbuf_tensor("s_td", [128, 2, 2, 1024], BF16) as s_td, \
             nc.sbuf_tensor("s_st4", [128, 8], F32) as s_st4, \
             nc.sbuf_tensor("s_nt", [128, 16], F32) as s_nt, \
             nc.sbuf_tensor("s_out", [128, D], F32) as s_out:
            q2T_view = s_h[:].rearrange("p c (k q) -> p c k q", q=128)
            t_hmult = {}
            prev_t_fg = None
            prev_t_dn = None
            for g in range(4):
                with ExitStack() as es_gu:
                    pGs = tuple(es_gu.enter_context(
                        nc.psum_tensor(f"pG{i}{g}", [128, 512], F32)) for i in range(4))
                    pUs = tuple(es_gu.enter_context(
                        nc.psum_tensor(f"pU{i}{g}", [128, 512], F32)) for i in range(4))
                    t_unit_mms = []
                    t_mm_c = {}
                    for oc in range(8):
                        for half in range(2):
                            un = oc * 2 + half
                            wpar = un % 2
                            w_g = []
                            if len(t_unit_mms) >= 2:
                                w_g.append(t_unit_mms[-2])
                            t_g = sch.dma("sync", f"tg{wpar}",
                                          lambda h, o=oc, hf=half, p=wpar: h.dma_start(
                                              out=s_tg[:, p, :, :],
                                              in_=tgT_v[:, hf * 4:(hf + 1) * 4,
                                                        o * 512:(o + 1) * 512]),
                                          waits=w_g)
                            t_u = sch.dma("sync", f"tu{wpar}",
                                          lambda h, o=oc, hf=half, p=wpar: h.dma_start(
                                              out=s_tu[:, p, :, :],
                                              in_=tuT_v[:, hf * 4:(hf + 1) * 4,
                                                        o * 512:(o + 1) * 512]),
                                          waits=w_g)
                            for c in range(4):
                                tc = g * 4 + c
                                for k4 in range(4):
                                    kc = half * 4 + k4
                                    w = []
                                    if k4 == 0 and c == 0:
                                        w += [t_g, t_u]
                                    if kc == 0:
                                        key = (oc - 1, c)
                                        if key in t_hmult:
                                            w.append(t_hmult[key])
                                        if oc == 0 and c == 0:
                                            w.append(t_c_done)
                                            if g > 0:
                                                w.append(prev_t_fg)
                                    sch.add("tensor",
                                            lambda h, k=kc, t_=tc, c_=c, wp=wpar, k4_=k4:
                                            h.matmul(
                                                pGs[c_][:],
                                                q1T_view[:, k, t_ * 128:(t_ + 1) * 128],
                                                s_tg[:, wp, k4_, :],
                                                start=(k == 0), stop=(k == KC - 1)),
                                            waits=w)
                                    sch.add("tensor",
                                            lambda h, k=kc, t_=tc, c_=c, wp=wpar, k4_=k4:
                                            h.matmul(
                                                pUs[c_][:],
                                                q1T_view[:, k, t_ * 128:(t_ + 1) * 128],
                                                s_tu[:, wp, k4_, :],
                                                start=(k == 0), stop=(k == KC - 1)),
                                            inc=(1 if (k4 == 3 and
                                                       (half == 1 or c == 3)) else 0))
                                if half == 1:
                                    t_mm_c[c] = ST("tensor")
                            t_unit_mms.append(ST("tensor"))
                        for c in range(4):
                            tc = g * 4 + c
                            ppar = c % 2
                            sch.add("scalar",
                                    lambda h, t_=tc, c_=c, p=ppar: h.activation(
                                        out=s_gtmp[:, p, :], in_=pGs[c_][:], func=AF.Silu,
                                        scale=s_r1g[:, t_:t_ + 1]),
                                    waits=([t_mm_c[c], ST("vector")] if c == 0
                                           else [t_mm_c[c], ST("vector"),
                                                 sch.chain("scalar")]))
                            t_ev = ST("scalar")
                            t_hmult[(oc, c)] = sch.add("vector",
                                    lambda h, c_=c, o=oc, p=ppar: h.tensor_tensor(
                                        s_h[:, c_, o * 512:(o + 1) * 512], s_gtmp[:, p, :],
                                        pUs[c_][:], ALU.mult),
                                    waits=[t_ev, sch.chain("vector")])
                            if oc == 7:
                                sch.add("vector", lambda h, c_=c: h.tensor_reduce(
                                    s_qs[:, 0:1], s_h[:, c_, :], AX, ALU.max,
                                    apply_absolute_value=True),
                                    waits=[sch.chain("vector")])
                                sch.add("vector", lambda h: h.tensor_scalar(
                                    s_qs[:, 0:1], s_qs[:, 0:1], 1e-5, None, ALU.max),
                                    waits=[sch.chain("vector")])
                                sch.add("vector", lambda h, c_=c: h.reciprocal(
                                    s_q2sc[:, c_:c_ + 1], s_qs[:, 0:1]),
                                    waits=[sch.chain("vector")])
                                sch.add("vector", lambda h, c_=c: h.tensor_scalar(
                                    s_q2sc[:, c_:c_ + 1], s_q2sc[:, c_:c_ + 1], 127.0, None,
                                    ALU.mult),
                                    waits=[sch.chain("vector")])
                                sch.add("vector", lambda h: h.tensor_scalar(
                                    s_qs[:, 2:3], s_qs[:, 0:1], 1.0 / 127.0, s_wsc[:, 4:5],
                                    ALU.mult, ALU.mult),
                                    waits=[sch.chain("vector")])
                                sch.add("vector", lambda h, t_=tc: h.tensor_tensor(
                                    s_r2d[:, t_:t_ + 1], s_qs[:, 2:3], s_r1u[:, t_:t_ + 1],
                                    ALU.mult),
                                    waits=[sch.chain("vector")])
                        if oc in (3, 6):
                            pop_B(1)
                sch.pe_fence()
                with nc.psum_tensor(f"ptD{g}", [128, 4, 1024], BF16) as ptD:
                    tok_tr = {}
                    for c in range(4):
                        tc = g * 4 + c
                        t_sub = {}
                        for osl in range(8):
                            qpar = osl % 2
                            sch.add("vector",
                                    lambda h, c_=c, o=osl: h.tensor_scalar(
                                        s_q2s[:], s_h[:, c_, o * 512:(o + 1) * 512],
                                        s_q2sc[:, c_:c_ + 1], MAGIC, ALU.mult, ALU.add),
                                    waits=[sch.chain("vector")])
                            w_sub = []
                            if c * 8 + osl >= 1:
                                w_sub.append(tok_tr[c * 8 + osl - 1])
                            w_sub.append(sch.chain("vector"))
                            t_sub[osl] = sch.add("vector",
                                    lambda h: h.tensor_scalar(
                                        s_q2b[:], s_q2s[:], MAGIC, None, ALU.subtract),
                                    waits=w_sub)
                            for j in range(4):
                                w = []
                                if j == 0:
                                    w.append(t_sub[osl])
                                    w.append(ST("scalar"))
                                sch.add("tensor",
                                        lambda h, j_=j: h.transpose(
                                            ptD[:, j_, 0:128], s_q2b[:, j_ * 128:(j_ + 1) * 128],
                                            s_ident[:]),
                                        waits=w, inc=(1 if j == 3 else 0))
                            t_tr = ST("tensor")
                            tok_tr[c * 8 + osl] = t_tr
                            sch.add("scalar",
                                    lambda h, c_=c, o=osl: h.activation(
                                        out=q2T_view[:, c_, o * 4:(o + 1) * 4, :],
                                        in_=ptD[:, :, 0:128], func=AF.Copy),
                                    waits=[t_tr, sch.chain("scalar")])
                t_q2ev = ST("scalar")
                pop_B(6 if g == 0 else 2)
                td_tok = {}
                for q in range(2):
                    w_dma = [prev_t_dn] if g > 0 else []
                    sch.dma("sync", f"td{q}",
                            lambda h, q_=q: h.dma_start(out=s_td[:, q_, :, :],
                                                        in_=tdT_v[:, q_, :, :]),
                            waits=w_dma)
                    td_tok[q] = (f"dma_td{q}", sch.counts[f"dma_td{q}"])
                with nc.psum_tensor(f"pD0{g}", [128, 1024], F32) as pD0, \
                     nc.psum_tensor(f"pD1{g}", [128, 1024], F32) as pD1, \
                     nc.psum_tensor(f"pD2{g}", [128, 1024], F32) as pD2, \
                     nc.psum_tensor(f"pD3{g}", [128, 1024], F32) as pD3:
                    pDs = (pD0, pD1, pD2, pD3)
                    for q in range(KI // 2):
                        par = q % 2
                        if q >= 2:
                            sch.dma("sync", f"td{par}",
                                    lambda h, q_=q, p=par: h.dma_start(
                                        out=s_td[:, p, :, :], in_=tdT_v[:, q_, :, :]),
                                    waits=[ST("tensor")])
                            t_td = (f"dma_td{par}", sch.counts[f"dma_td{par}"])
                        else:
                            t_td = td_tok[q]
                        for sub in range(2):
                            kc = q * 2 + sub
                            for c in range(4):
                                for hf in range(2):
                                    sch.add("tensor",
                                            lambda h, c_=c, k=kc, p=par, s_=sub, f=hf: h.matmul(
                                                pDs[c_][:, f * 512:(f + 1) * 512],
                                                q2T_view[:, c_, k, :],
                                                s_td[:, p, s_, f * 512:(f + 1) * 512],
                                                start=(k == 0), stop=(k == KI - 1)),
                                            waits=(([t_td, t_q2ev] if q == 0 else [t_td])
                                                   if (c == 0 and hf == 0 and sub == 0) else []),
                                            inc=(1 if (c, hf, sub) == (3, 1, 1) else 0))
                    t_dn = ST("tensor")
                    for c in range(4):
                        tc = g * 4 + c
                        sch.add("scalar",
                                lambda h, c_=c, t_=tc: h.activation(
                                    out=f_ap(t_), in_=pDs[c_][:], func=AF.Identity,
                                    scale=s_r2d[:, t_:t_ + 1]),
                                waits=([t_dn] if c == 0 else []) + [sch.chain("scalar")])
                    t_fg = ST("scalar")
                for c in range(4):
                    tc = g * 4 + c
                    sch.add("scalar",
                            lambda h, t_=tc, c_=c: h.activation(
                                out=s_h[:, c_, 0:D], in_=f_ap(t_), func=AF.Identity,
                                accum_out=s_st4[:, c_:c_ + 1]),
                            waits=[t_fg, sch.chain("scalar")])
                    sch.add("scalar",
                            lambda h, t_=tc, c_=c: h.activation(
                                out=s_h[:, c_, 0:D], in_=f_ap(t_), func=AF.Square,
                                accum_out=s_st4[:, 4 + c_:5 + c_]),
                            waits=[sch.chain("scalar")])
                t_sums = ST("scalar")
                sch.add("vector", lambda h: h.tensor_scalar(
                    s_nt[:, 0:4], s_st4[:, 0:4], 1.0 / D, None, ALU.mult),
                    waits=[t_sums, sch.chain("vector")])
                sch.add("vector", lambda h: h.tensor_tensor(
                    s_nt[:, 4:8], s_st4[:, 0:4], s_nt[:, 0:4], ALU.mult),
                    waits=[sch.chain("vector")])
                sch.add("vector", lambda h: h.tensor_tensor(
                    s_nt[:, 4:8], s_st4[:, 4:8], s_nt[:, 4:8], ALU.subtract),
                    waits=[sch.chain("vector")])
                sch.add("vector", lambda h: h.tensor_scalar(
                    s_nt[:, 4:8], s_nt[:, 4:8], 1.0 / D, EPS, ALU.mult, ALU.add),
                    waits=[sch.chain("vector")])
                iv = s_nt[:, 4:8].bitcast(I32)
                iy = s_nt[:, 8:12].bitcast(I32)
                sch.add("vector", lambda h: h.tensor_scalar(
                    iy, iv, 1, None, ALU.logical_shift_right),
                    waits=[sch.chain("vector")])
                sch.add("vector", lambda h: h.tensor_scalar(
                    iy, iy, -1, 0x5F3759DF, ALU.mult, ALU.add),
                    waits=[sch.chain("vector")])
                for _ in range(3):
                    sch.add("vector", lambda h: h.tensor_tensor(
                        s_nt[:, 12:16], s_nt[:, 8:12], s_nt[:, 8:12], ALU.mult),
                        waits=[sch.chain("vector")])
                    sch.add("vector", lambda h: h.tensor_tensor(
                        s_nt[:, 12:16], s_nt[:, 12:16], s_nt[:, 4:8], ALU.mult),
                        waits=[sch.chain("vector")])
                    sch.add("vector", lambda h: h.tensor_scalar(
                        s_nt[:, 12:16], s_nt[:, 12:16], -0.5, 1.5, ALU.mult, ALU.add),
                        waits=[sch.chain("vector")])
                    sch.add("vector", lambda h: h.tensor_tensor(
                        s_nt[:, 8:12], s_nt[:, 8:12], s_nt[:, 12:16], ALU.mult),
                        waits=[sch.chain("vector")])
                sch.add("vector", lambda h: h.tensor_tensor(
                    s_nt[:, 0:4], s_nt[:, 0:4], s_nt[:, 8:12], ALU.mult),
                    waits=[sch.chain("vector")])
                sch.add("vector", lambda h: h.tensor_scalar(
                    s_nt[:, 0:4], s_nt[:, 0:4], -1.0, None, ALU.mult),
                    waits=[sch.chain("vector")])
                t_rs = ST("vector")
                for c in range(4):
                    tc = g * 4 + c
                    w_ap = [t_rs, sch.chain("scalar")]
                    if tc >= 1:
                        w_ap.append(("dma_out0", sch.counts.get("dma_out0", 0)))
                    sch.add("scalar",
                            lambda h, t_=tc, c_=c: h.activation(
                                out=s_out[:], in_=f_ap(t_), func=AF.Identity,
                                scale=s_nt[:, 8 + c_:9 + c_], bias=s_nt[:, c_:c_ + 1]),
                            waits=w_ap)
                    t_ap = ST("scalar")
                    sch.add("vector",
                            lambda h, t_=tc: h.tensor_tensor(
                                s_out[:], s_out[:], s_x1[:, t_, :], ALU.add),
                            waits=[t_ap, sch.chain("vector")])
                    t_o = ST("vector")
                    sch.dma("gpsimd", "out0",
                            lambda h, t_=tc: h.dma_start(out=out_r[:, t_, :],
                                                         in_=s_out[:]),
                            waits=[t_o])
                prev_t_fg = t_fg
                prev_t_dn = t_dn
                if g < 3:
                    t_c_done = emit_C_q(g + 1, w_first=[t_y_q[g + 1], t_fg])
            sch.add("gpsimd", None,
                    waits=[("dma_out0", sch.counts["dma_out0"])])

        es_w_slab.close()
        if True:
            return _finish()
        with nc.sbuf_tensor("lnscT", [128, D], F32) as lnscT, \
             nc.sbuf_tensor("s_out", [128, D], F32) as s_out:
            for tc in range(NC):
                t_st = emit_lnstats(f_view[:, tc, :], lnscT)
                sch.add("scalar",
                        lambda h, t_=tc: h.activation(out=lnscT[:], in_=f_view[:, t_, :],
                                                      func=AF.Identity, scale=s_stats[:, 4:5],
                                                      bias=s_stats[:, 5:6]),
                        waits=[t_st], inc=1)
                t_ln4 = ST("scalar")
                par = tc % 2
                w = [t_ln4]
                if tc >= 2:
                    w.append((f"dma_out{par}", sch.counts.get(f"dma_out{par}", 0)))
                sch.add("vector",
                        lambda h, t_=tc, p=par: h.tensor_tensor(
                            s_out[:, p, :], lnscT[:], s_x1[:, t_, :], ALU.add),
                        waits=w + [sch.chain("vector")])
                t_o = ST("vector")
                sch.dma("sync", f"out{par}",
                        lambda h, t_=tc, p=par: h.dma_start(out=out_r[:, t_, :],
                                                            in_=s_out[:, p, :]),
                        waits=[t_o])
                sch.add("scalar", None, waits=[t_o])
            sch.add("gpsimd", None,
                    waits=[("dma_out0", sch.counts["dma_out0"]),
                           ("dma_out1", sch.counts["dma_out1"])])

        return _finish()


_NC_CACHE = None


def _get_nc():
    global _NC_CACHE
    if _NC_CACHE is None:
        _NC_CACHE = build_nc()
    return _NC_CACHE


def kernel(L0_attn_x, L0_attn_Wd, L0_attn_bd, L0_attn_a_log, L0_attn_b_param,
           L0_attn_Wo, L0_attn_bo, L0_ffn_w_gate, L0_ffn_w_up, L0_ffn_w_down):
    nc = _get_nc()
    x = np.ascontiguousarray(np.asarray(L0_attn_x, dtype=np.float32))
    wd = np.ascontiguousarray(np.asarray(L0_attn_Wd, dtype=np.float32))
    bd = np.ascontiguousarray(np.asarray(L0_attn_bd, dtype=np.float32).reshape(KC, 128).T)
    alog = np.ascontiguousarray(np.asarray(L0_attn_a_log, dtype=np.float32)
                                .reshape(KC, 128, NSTATE).transpose(1, 0, 2).reshape(128, KC * NSTATE))
    bprm = np.ascontiguousarray(np.asarray(L0_attn_b_param, dtype=np.float32)
                                .reshape(KC, 128, NSTATE).transpose(1, 0, 2).reshape(128, KC * NSTATE))
    wo = np.ascontiguousarray(np.asarray(L0_attn_Wo, dtype=np.float32))
    borep = np.ascontiguousarray(np.broadcast_to(np.asarray(L0_attn_bo, dtype=np.float32), (128, D)))
    wg = np.ascontiguousarray(np.asarray(L0_ffn_w_gate, dtype=np.float32))
    wu = np.ascontiguousarray(np.asarray(L0_ffn_w_up, dtype=np.float32))
    wdn = np.ascontiguousarray(np.asarray(L0_ffn_w_down, dtype=np.float32))
    ident = np.eye(128, dtype=ml_dtypes.bfloat16)
    constv = np.ascontiguousarray(np.broadcast_to(
        np.array([MAGIC, EPS, -0.5, -MAGIC], dtype=np.float32), (128, 4)))
    onesk = np.ones((1, 128), dtype=np.float32)
    onesp = np.ones((128, 1), dtype=np.float32)

    in_maps = []
    for b in range(8):
        in_maps.append({
            "x": np.ascontiguousarray(x[b]),
            "wd": wd, "bd": bd, "alog": alog, "bprm": bprm,
            "wo": wo, "borep": borep,
            "wg": wg, "wu": wu, "wdn": wdn,
            "ident": ident, "onesk": onesk, "onesp": onesp, "constv": constv,
        })
    res = run_bass_kernel_spmd(nc, in_maps, core_ids=list(range(8)))
    out = np.stack([res.results[b]["out"] for b in range(8)], axis=0)
    return out.astype(np.float32)
